# version 15 (final)
# baseline (speedup 1.0000x reference)
import contextlib
import os
import numpy as np
import concourse.bass as bass
import concourse.mybir as mybir
import concourse.bass_utils as _bu
from concourse.bass_utils import run_bass_kernel_spmd

if not getattr(_bu, "_ldwopt_patched", False):
    _orig_run_command = _bu.run_command

    def _run_command_ldwopt(cmd, *a, **kw):
        try:
            cmd = ["--enable-ldw-opt=true" if c == "--enable-ldw-opt=false"
                   else c for c in cmd]
        except TypeError:
            pass
        return _orig_run_command(cmd, *a, **kw)

    _bu.run_command = _run_command_ldwopt
    _bu._ldwopt_patched = True

F32 = mybir.dt.float32
BF16 = mybir.dt.bfloat16
AF = mybir.ActivationFunctionType
ALU = mybir.AluOpType
AXX = mybir.AxisListType.X

NP = 128
S = 2048
D = 1024
NT = S // NP
NDC = D // NP
NB = 4
PEW = 288
PED = S * PEW
EPS_LN = 1e-5
EPS_AT = 1e-8
EPS_FF = 1e-5
_KPH = int(os.environ.get("KPH", "3"))
_KNP = int(os.environ.get("KNP", "8"))
_KQB = int(os.environ.get("KQB", "8"))


class Plan:
    ES = ("sync", "tensor", "vector", "scalar", "gpsimd")

    def __init__(self):
        self.ops = {e: [] for e in self.ES}
        self.pos = {e: 0 for e in self.ES}
        self.dma = {}
        self.pending = {e: () for e in self.ES}

    def c(self, eng, fn, waits=()):
        w = tuple(self.pending[eng]) + tuple(t for t in waits if t is not None)
        self.pending[eng] = ()
        self.ops[eng].append((fn, w, ("@" + eng, 1)))
        self.pos[eng] += 1
        return ("@" + eng, self.pos[eng])

    def d(self, eng, fn, sem, waits=()):
        w = tuple(self.pending[eng]) + tuple(t for t in waits if t is not None)
        if eng == "gpsimd" and self.dma.get(sem, 0) > 0:
            w = w + ((sem, self.dma[sem]),)
        self.pending[eng] = ()
        self.dma[sem] = self.dma.get(sem, 0) + 16
        self.ops[eng].append((fn, w, (sem, 16)))
        return (sem, self.dma[sem])

    def at(self, eng):
        return ("@" + eng, self.pos[eng])

    def barrier(self):
        snap = tuple(("@" + e, self.pos[e]) for e in ("tensor", "vector", "scalar")
                     if self.pos[e] > 0)
        for e in self.ES:
            self.pending[e] = snap


def cpy(e, out, in_):
    if hasattr(e, "tensor_copy"):
        return e.tensor_copy(out=out, in_=in_)
    return e.copy(out=out, in_=in_)


def evac_segments(i):
    q0 = 128 * i
    wlo = max(q0 - 16, 0)
    whi = min(q0 + 144, S)
    blocks = []
    for b in range(NB):
        s0, s1 = 512 * b, 512 * b + 512
        segs = []
        if min(s1, wlo) > s0:
            segs.append((s0, min(s1, wlo), "lo"))
        if min(s1, whi) > max(s0, wlo):
            segs.append((max(s0, wlo), min(s1, whi), "win"))
        if s1 > max(s0, whi):
            segs.append((max(s0, whi), s1, "hi"))
        blocks.append(segs)
    return wlo, whi, blocks


def build():
    nc = bass.Bass("TRN2", target_bir_lowering=False)

    x_d = nc.dram_tensor("x", [S, D], F32, kind="ExternalInput").ap()
    wat_d = nc.dram_tensor("wat", [D, 3 * D], F32, kind="ExternalInput").ap()
    bat_d = nc.dram_tensor("bat", [NP, 24], F32, kind="ExternalInput").ap()
    relT_d = nc.dram_tensor("relT", [NP, 33], F32, kind="ExternalInput").ap()
    snb_d = nc.dram_tensor("snb", [1, 1], F32, kind="ExternalInput").ap()
    wot_d = nc.dram_tensor("wot", [D, D], F32, kind="ExternalInput").ap()
    bot_d = nc.dram_tensor("bot", [NP, 8], F32, kind="ExternalInput").ap()
    bav_d = nc.dram_tensor("bavd", [1, D], F32, kind="ExternalInput").ap()
    wft_d = nc.dram_tensor("wft", [D, D], F32, kind="ExternalInput").ap()
    ident_d = nc.dram_tensor("ident", [NP, NP], F32, kind="ExternalInput").ap()
    onesr_d = nc.dram_tensor("onesr", [1, NP], F32, kind="ExternalInput").ap()
    out_d = nc.dram_tensor("out", [S, D], F32, kind="ExternalOutput").ap()
    pe_d = nc.dram_tensor("pedram", [32 * PED], F32, kind="Internal").ap()
    ctx_d = nc.dram_tensor("ctxd", [D, S], F32, kind="Internal").ap()

    es = contextlib.ExitStack()
    Ah = es.enter_context(nc.sbuf_tensor("arnAh", [NP, 16384], BF16))
    Al = es.enter_context(nc.sbuf_tensor("arnAl", [NP, 16384], BF16))
    B = es.enter_context(nc.sbuf_tensor("arnB", [NP, 16384], F32))
    C = es.enter_context(nc.sbuf_tensor("arnC", [NP, 13696], F32))
    W = es.enter_context(nc.sbuf_tensor("wspl", [NP, 8192], BF16))
    ids = es.enter_context(nc.sbuf_tensor("ids", [NP, NP], F32))
    cst = es.enter_context(nc.sbuf_tensor("cst", [NP, 64], F32))
    sc1 = es.enter_context(nc.sbuf_tensor("sc1", [1, 8], F32))
    onr = es.enter_context(nc.sbuf_tensor("onr", [1, NP], F32))
    bav = es.enter_context(nc.sbuf_tensor("bav", [1, D], F32))
    rlt = es.enter_context(nc.sbuf_tensor("rlt", [NP, 33], F32))
    ps = [es.enter_context(nc.psum_tensor(f"ps{i}", [NP, 512], F32))
          for i in range(8)]
    semnames = ["@tensor", "@vector", "@scalar",
                "cX0", "cX1", "cX2", "cX3", "cC", "cWq", "cWk", "cWv",
                "cPt0", "cPt1", "cCr0", "cCr1", "cXr0", "cXr1",
                "cWo0", "cWo1", "cWf0", "cWf1",
                "gPe0", "gPe1", "gCx0", "gCx1", "gOut0", "gOut1"]
    sems = {n: es.enter_context(nc.semaphore(n.replace("@", "e_")))
            for n in semnames}

    hAh = Ah[0:1, 0:1].tensor
    hAl = Al[0:1, 0:1].tensor
    hC_ = C[0:1, 0:1].tensor
    hB_ = B[0:1, 0:1].tensor
    VOF = 16 * PED
    Bb = B[:, 12288:16384].bitcast(BF16)

    P = Plan()

    QTMP, KTMP, VG, TSL = 0, 2048, 4096, 8256
    QH, QL, KH, KL = 0, 2048, 4096, 6144
    XT, LNS, XNT, RED = 0, 2048, 3072, 4096
    PTK, WST, PET, PAT = 4200, 5256, 7304, 9608
    NPT, CTM, RSR, XQ, CSP, RSB = 11656, 11784, 12296, 12808, 13064, 13576
    OFM, WOS, CTX2, XRE, LNS2, XNT2, RED2 = 0, 4096, 6144, 7168, 9216, 10240, 11264
    WFS, SCR, OFM3, OTL, RED3 = 0, 2048, 3072, 7168, 9216
    WQH, WQL, WKH, WKL, WVH, WVL = 0, 1024, 2048, 3072, 4096, 6144
    BAB, BOB, ONEC, SNB, BQS, SSB, THR, NTHR, ACC, EPSC, D03C = (
        0, 24, 32, 33, 34, 35, 36, 37, 38, 39, 40)

    def cc(col, n=1):
        return cst[:, col:col + n]

    def split(hi_eng, hi_ap, lo_ap, src_ap, waits):
        if hi_eng == "scalar":
            ht = P.c("scalar", (lambda e, hi_ap=hi_ap, src_ap=src_ap:
                     e.activation(out=hi_ap, in_=src_ap, func=AF.Identity)),
                     waits=waits)
        else:
            ht = P.c("vector", (lambda e, hi_ap=hi_ap, src_ap=src_ap:
                     e.tensor_copy(out=hi_ap, in_=src_ap)), waits=waits)
        lt = P.c("vector", (lambda e, lo_ap=lo_ap, src_ap=src_ap, hi_ap=hi_ap:
                 e.tensor_tensor(lo_ap, src_ap, hi_ap, ALU.subtract)),
                 waits=(ht,))
        return ht, lt

    def ln_tile(x_ap, red, lns, xnt, extra_r3_waits, r1_waits):
        r1 = P.c("vector", (lambda e: e.tensor_reduce(
            C[:, red:red + 1], x_ap, AXX, ALU.add)), waits=r1_waits)
        r2 = P.c("scalar", (lambda e: e.activation(
            out=C[:, red + 1:red + 2], in_=C[:, red:red + 1],
            func=AF.Identity, scale=-1.0 / D)), waits=(r1,))
        r3 = P.c("vector", (lambda e: e.tensor_scalar(
            C[:, xnt:xnt + 1024], x_ap, C[:, red + 1:red + 2], None, ALU.add)),
            waits=(r2,) + tuple(extra_r3_waits))
        r4 = P.c("scalar", (lambda e: e.activation(
            out=C[:, lns:lns + 1024], in_=C[:, xnt:xnt + 1024], func=AF.Square,
            accum_out=C[:, red + 2:red + 3])), waits=(r3,))
        r5 = P.c("scalar", (lambda e: e.activation(
            out=C[:, red + 3:red + 4], in_=C[:, red + 2:red + 3],
            func=AF.Identity, scale=1.0 / D, bias=cc(EPSC))),
            waits=(r4, epsc_tok))
        r6 = P.c("scalar", (lambda e: e.activation(
            out=C[:, red + 4:red + 5], in_=C[:, red + 3:red + 4],
            func=AF.Sqrt)), waits=(r5,))
        r7 = P.c("vector", (lambda e: e.reciprocal(
            C[:, red + 5:red + 6], C[:, red + 4:red + 5])), waits=(r6,))
        r8 = P.c("vector", (lambda e: e.tensor_mul(
            C[:, red + 6:red + 7], C[:, red + 3:red + 4], C[:, red + 5:red + 6])),
            waits=(r7,))
        r9 = P.c("vector", (lambda e: e.tensor_mul(
            C[:, red + 7:red + 8], C[:, red + 6:red + 7], C[:, red + 5:red + 6])),
            waits=(r8,))
        r10 = P.c("vector", (lambda e: e.tensor_scalar(
            C[:, red + 8:red + 9], C[:, red + 7:red + 8], -0.5, 1.5,
            ALU.mult, ALU.add)), waits=(r9,))
        r11 = P.c("vector", (lambda e: e.tensor_mul(
            C[:, red + 9:red + 10], C[:, red + 8:red + 9], C[:, red + 5:red + 6])),
            waits=(r10,))
        r12 = P.c("vector", (lambda e: e.tensor_scalar(
            C[:, xnt:xnt + 1024], C[:, xnt:xnt + 1024],
            C[:, red + 9:red + 10], None, ALU.mult)), waits=(r11, r4))
        return r12

    def xn_transposes(tt, xnt, r12, bankfree):
        toks = []
        for gg in range(2):
            bank = ps[4 + gg]
            tws = [r12]
            if (4 + gg) in bankfree:
                tws.append(bankfree[4 + gg])
            tp = None
            for jj in range(4):
                dc = gg * 4 + jj
                tp = P.c("tensor", (lambda e, bank=bank, jj=jj, dc=dc, xnt=xnt:
                         e.transpose(
                    bank[:, jj * 128:(jj + 1) * 128],
                    C[:, xnt + dc * 128: xnt + (dc + 1) * 128], ids[:])),
                    waits=tuple(tws) if jj == 0 else ())
            dims = [[16384, NP], [2048, 4], [1, 128]]
            off = gg * 4 * 2048 + tt * 128
            hi_ap = bass.AP(hAh, off, dims)
            lo_ap = bass.AP(hAl, off, dims)
            src = bass.AP(bank[0:1, 0:1].tensor, 0, [[512, NP], [128, 4], [1, 128]])
            ht, lt = split("scalar" if gg == 0 else "vector",
                           hi_ap, lo_ap, src, (tp,))
            bankfree[4 + gg] = lt
            toks += [ht, lt]
        return toks

    P.d("sync", lambda e: e.dma_start(out=ids[:], in_=ident_d), "cC")
    P.d("sync", lambda e: e.dma_start(out=onr[:], in_=onesr_d), "cC")
    P.d("sync", lambda e: e.dma_start(out=rlt[:], in_=relT_d), "cC")
    P.d("sync", lambda e: e.dma_start(out=sc1[0:1, 0:1], in_=snb_d), "cC")
    P.d("sync", lambda e: e.dma_start(out=cc(BAB, 24), in_=bat_d), "cC")
    P.d("sync", lambda e: e.dma_start(out=cc(BOB, 8), in_=bot_d), "cC")
    P.d("sync", lambda e: e.dma_start(out=bav[0:1, :], in_=bav_d), "cC")
    cC_tok = ("cC", P.dma["cC"])

    onec_tok = P.c("vector", lambda e: e.memset(cc(ONEC), 1.0))
    epsc_tok = P.c("vector", lambda e: e.memset(cc(EPSC), EPS_LN))
    pet_ms = P.c("vector", lambda e: e.memset(C[:, PET:PET + 2304], 0.0))
    d03_tok = P.c("vector", lambda e: e.tensor_tensor(
        cc(D03C), rlt[:, 0:1], rlt[:, 32:33], ALU.subtract), waits=(cC_tok,))
    vone_tok = P.c("vector", lambda e: e.memset(
        bass.AP(hB_, VG + 64, [[16384, NP], [260, 16], [65, 4]]), 1.0))
    t_snb = P.c("tensor", lambda e: e.matmul(
        ps[7][:, 0:1], onr[0:1, :], sc1[0:1, 0:1], start=True, stop=True),
        waits=(cC_tok,))
    snb_ev = P.c("vector", lambda e: e.tensor_copy(out=cc(SNB), in_=ps[7][:, 0:1]),
                 waits=(t_snb,))

    bankfree = {}
    lastxc = {}
    for tt in range(NT):
        slot = tt % 4
        ld = P.d("sync", (lambda e, tt=tt, slot=slot: e.dma_start(
            out=B[:, slot * 1024:(slot + 1) * 1024],
            in_=x_d[tt * NP:(tt + 1) * NP, :])), f"cX{slot}",
            waits=(lastxc.get(slot),))
        x_ap = B[:, slot * 1024:(slot + 1) * 1024]
        prevT = (P.at("tensor"),) if tt > 0 else ()
        r12 = ln_tile(x_ap, RED, LNS, XNT, prevT, (ld,))
        lastxc[slot] = r12
        xn_transposes(tt, XNT, r12, bankfree)

    P.barrier()

    def ah(dc, c0, n):
        return Ah[:, dc * 2048 + c0: dc * 2048 + c0 + n]

    def al(dc, c0, n):
        return Al[:, dc * 2048 + c0: dc * 2048 + c0 + n]

    pe_dma_tok = {}
    pat_tok = {}
    patch_used = {}
    csp_free = {}
    vg_ready = None
    prev_tail = None
    ctx_written = []
    pre_w = {}

    def load_pair_w(pair, wqkf, wvf, stg_free=()):
        g = pair // 2
        wq_ld = P.d("sync", (lambda e, pair=pair: e.dma_start(
            out=C[:, WST:WST + 1024].rearrange("p (a b) -> p a b", a=8),
            in_=bass.AP(wat_d.tensor, pair * 128,
                        [[3 * D, NP], [3 * D * NP, 8], [1, 128]]))), "cWq",
            waits=(wqkf,) + tuple(stg_free))
        wk_ld = P.d("sync", (lambda e, pair=pair: e.dma_start(
            out=C[:, WST + 1024:WST + 2048].rearrange("p (a b) -> p a b", a=8),
            in_=bass.AP(wat_d.tensor, D + pair * 128,
                        [[3 * D, NP], [3 * D * NP, 8], [1, 128]]))), "cWk",
            waits=(wqkf,) + tuple(stg_free))
        qh_, ql_ = split("scalar", W[:, WQH:WQH + 1024], W[:, WQL:WQL + 1024],
                         C[:, WST:WST + 1024], (wq_ld, wqkf))
        kh_, kl_ = split("scalar", W[:, WKH:WKH + 1024], W[:, WKL:WKL + 1024],
                         C[:, WST + 1024:WST + 2048], (wk_ld, wqkf))
        vh_ = vl_ = None
        if pair % 2 == 0:
            wv_ld = P.d("sync", (lambda e, g=g: e.dma_start(
                out=C[:, WST:WST + 2048].rearrange("p (a b) -> p a b", a=8),
                in_=bass.AP(wat_d.tensor, 2 * D + g * 256,
                            [[3 * D, NP], [3 * D * NP, 8], [1, 256]]))), "cWv",
                waits=(ql_, kl_, qh_, kh_, wvf))
            vh_, vl_ = split("scalar", W[:, WVH:WVH + 2048], W[:, WVL:WVL + 2048],
                             C[:, WST:WST + 2048], (wv_ld, wvf))
        return (qh_, ql_, kh_, kl_, vh_, vl_)

    prev_qdone = None
    for pair in range(8 if _KPH >= 1 else 0):
        if pair >= _KNP:
            break
        hA = 2 * pair
        g = pair // 2

        if pair == 0:
            pre_w[0] = load_pair_w(0, None, None)
        wqh_t, wql_t, wkh_t, wkl_t, wvh_t, wvl_t = pre_w[pair]
        if wvh_t is None:
            wvh_t, wvl_t = pre_w[pair - 1][4], pre_w[pair - 1][5]

        bq_tok = P.c("vector", (lambda e, pair=pair: e.tensor_scalar(
            cc(BQS), cst[:, BAB + pair:BAB + pair + 1], 0.125, None, ALU.mult)),
            waits=(cC_tok, prev_qdone))

        qk_evacs = []
        for which, whofs, wlofs, wsp in (
                ("q", WQH, WQL, (wqh_t, wql_t)),
                ("k", WKH, WKL, (wkh_t, wkl_t))):
            for qb in range(4):
                bank = ps[qb % 2]
                mws = list(wsp)
                if len(qk_evacs) >= 2:
                    mws.append(qk_evacs[-2])
                mm = None
                first = True
                for dc in range(NDC):
                    for (wo, xf) in ((whofs, ah), (wlofs, ah), (whofs, al)):
                        last = (dc == NDC - 1 and wo == whofs and xf is al)
                        mm = P.c("tensor", (lambda e, bank=bank, wo=wo, dc=dc,
                                            qb=qb, xf=xf, first=first, last=last:
                                 e.matmul(
                            bank[:, 0:512],
                            W[:, wo + dc * 128: wo + (dc + 1) * 128],
                            xf(dc, qb * 512, 512),
                            start=first, stop=last)),
                            waits=tuple(mws) if first else ())
                        first = False
                if which == "q":
                    ev = P.c("scalar", (lambda e, bank=bank, qb=qb:
                             e.activation(
                        out=B[:, QTMP + qb * 512: QTMP + qb * 512 + 512],
                        in_=bank[:, 0:512], func=AF.Identity,
                        bias=cc(BQS), scale=0.125)), waits=(mm, bq_tok))
                else:
                    ev = P.c("scalar", (lambda e, bank=bank, qb=qb, pair=pair:
                             e.activation(
                        out=B[:, KTMP + qb * 512: KTMP + qb * 512 + 512],
                        in_=bank[:, 0:512], func=AF.Identity,
                        bias=cst[:, BAB + 8 + pair:BAB + 9 + pair], scale=1.0)),
                        waits=(mm,))
                qk_evacs.append(ev)
        q_done = qk_evacs[3]
        k_done = qk_evacs[7]
        prev_qdone = q_done
        kb0 = P.c("vector", (lambda e: e.tensor_scalar(
            B[:, KTMP:KTMP + 128], B[:, KTMP:KTMP + 128],
            rlt[:, 0:1], None, ALU.add)), waits=(k_done, cC_tok))
        kb1 = P.c("vector", (lambda e: e.tensor_scalar(
            B[:, KTMP + 128:KTMP + 2048], B[:, KTMP + 128:KTMP + 2048],
            rlt[:, 32:33], None, ALU.add)))
        qsp = split("scalar", Bb[:, QH:QH + 2048], Bb[:, QL:QL + 2048],
                    B[:, QTMP:QTMP + 2048], (q_done,))
        ksp = split("scalar", Bb[:, KH:KH + 2048], Bb[:, KL:KL + 2048],
                    B[:, KTMP:KTMP + 2048], (kb0, kb1))
        qk_split = (qsp[0], qsp[1], ksp[0], ksp[1])

        if pair % 2 == 0:
            vprev = None
            for tt in range(NT):
                bank = ps[6 + tt % 2]
                mws = [wvh_t, wvl_t, cC_tok, onec_tok, vprev]
                first = True
                for dc in range(NDC):
                    for (wo, xf) in ((WVH, ah), (WVH, al), (WVL, ah)):
                        P.c("tensor", (lambda e, bank=bank, dc=dc, tt=tt,
                                       wo=wo, xf=xf, first=first: e.matmul(
                            bank[:, 0:256],
                            xf(dc, tt * 128, 128),
                            W[:, wo + dc * 256: wo + (dc + 1) * 256],
                            start=first, stop=False)),
                            waits=tuple(mws) if first else ())
                        first = False
                mm = P.c("tensor", (lambda e, bank=bank, g=g: e.matmul(
                    bank[:, 0:256], onr[0:1, :],
                    bav[0:1, g * 256:(g + 1) * 256],
                    start=False, stop=True)))
                vprev = P.c("vector", (lambda e, bank=bank, tt=tt: e.tensor_copy(
                    out=bass.AP(hB_, VG + tt * 260,
                                [[16384, NP], [65, 4], [1, 64]]),
                    in_=bank[:, 0:256])), waits=(mm,))
            vg_ready = vprev
        v_use_done = P.at("tensor")

        if pair < _KNP - 1 and pair < 7:
            nwv = v_use_done if (pair + 1) % 2 == 0 else None
            pre_w[pair + 1] = load_pair_w(
                pair + 1, v_use_done, nwv,
                stg_free=tuple(t for t in pre_w[pair] if t is not None))

        ptprev = {}
        ptok_ev = []
        for tt in range(NT):
            for hh in range(2):
                bank = 4 + (tt * 2 + hh) % 2
                mws = [q_done, cC_tok, ptprev.get(bank), prev_tail]
                mm = P.c("tensor", (lambda e, bank=bank, hh=hh, tt=tt: e.matmul(
                    ps[bank][:, 0:33],
                    B[:, QTMP + tt * 128: QTMP + (tt + 1) * 128][
                        hh * 64:(hh + 1) * 64, :],
                    rlt[hh * 64:(hh + 1) * 64, :],
                    start=True, stop=True)), waits=tuple(mws))
                peng = "vector" if (tt * 2 + hh) % 2 == 0 else "scalar"
                ev = P.c(peng, (lambda e, bank=bank, tt=tt, hh=hh: cpy(
                    e, C[:, PTK + (tt * 2 + hh) * 33:
                         PTK + (tt * 2 + hh) * 33 + 33],
                    ps[bank][:, 0:33])), waits=(mm,))
                ptprev[bank] = ev
                ptok_ev.append(ev)
        ptok_done = ptok_ev[-1]
        nx1 = P.c("vector", (lambda e: e.tensor_scalar(
            C[:, NPT:NPT + 32],
            bass.AP(hC_, PTK, [[13696, NP], [33, 32]]),
            -1.0, None, ALU.mult)),
            waits=(ptok_done, ptok_ev[-2]))
        nx2 = P.c("vector", (lambda e: e.tensor_scalar(
            C[:, NPT + 32:NPT + 64],
            bass.AP(hC_, PTK + 32, [[13696, NP], [33, 32]]),
            -1.0, None, ALU.mult)),
            waits=(ptok_done, ptok_ev[-2]))
        nx3 = P.c("vector", (lambda e: e.tensor_tensor(
            C[:, NPT + 64:NPT + 96],
            bass.AP(hC_, PTK + 32, [[13696, NP], [33, 32]]),
            bass.AP(hC_, PTK, [[13696, NP], [33, 32]]),
            ALU.subtract)), waits=(ptok_done, ptok_ev[-2]))
        npx_tok = P.c("vector", (lambda e: e.tensor_scalar(
            C[:, NPT + 96:NPT + 128],
            C[:, NPT + 64:NPT + 96], -1.0, None, ALU.mult)),
            waits=(nx1, nx2, nx3))

        def pe_build(tt):
            slot = tt % 2
            base = PET + slot * 1152
            prev = pe_dma_tok.get((pair, tt - 2)) or (
                pe_dma_tok.get((pair - 1, NT - 2 + tt)) if tt < 2 else None)
            bt = None
            for hh in range(2):
                idx = tt * 2 + hh
                pcol = PTK + idx * 33
                n0c = C[:, NPT + idx:NPT + idx + 1]
                n32c = C[:, NPT + 32 + idx:NPT + 33 + idx]
                d32c = C[:, NPT + 64 + idx:NPT + 65 + idx]
                nd32c = C[:, NPT + 96 + idx:NPT + 97 + idx]
                b0 = base + hh * 288
                b2 = base + 576 + hh * 288
                bws = ((npx_tok, pet_ms, prev, ptok_ev[tt * 2],
                        ptok_ev[tt * 2 + 1]) if hh == 0 else ())
                bt = P.c("scalar", (lambda e, b0=b0, pcol=pcol, n0c=n0c:
                         e.activation(
                    out=C[:, b0 + 127:b0 + 160], in_=C[:, pcol:pcol + 33],
                    func=AF.Identity, bias=n0c, scale=1.0)),
                    waits=tuple(w for w in bws if w))
                bt = P.c("scalar", (lambda e, b0=b0, d32c=d32c: e.activation(
                    out=C[:, b0 + 160:b0 + 288], in_=C[:, b0 + 160:b0 + 288],
                    func=AF.Identity, bias=d32c, scale=0.0)))
                bt = P.c("scalar", (lambda e, b2=b2, nd32c=nd32c: e.activation(
                    out=C[:, b2:b2 + 127], in_=C[:, b2:b2 + 127],
                    func=AF.Identity, bias=nd32c, scale=0.0)))
                bt = P.c("scalar", (lambda e, b2=b2, pcol=pcol, n32c=n32c:
                         e.activation(
                    out=C[:, b2 + 127:b2 + 160], in_=C[:, pcol:pcol + 33],
                    func=AF.Identity, bias=n32c, scale=1.0)))
            P.d("gpsimd", (lambda e, base=base, hA=hA, tt=tt: e.dma_start(
                out=bass.AP(pe_d.tensor, hA * PED + tt * 128 * PEW,
                            [[PEW, NP], [PED, 2], [1, PEW]]),
                in_=C[:, base:base + 576].rearrange("p (a b) -> p a b", a=2))),
                f"gPe{slot}", waits=(bt,))
            tok = P.d("gpsimd", (lambda e, base=base, hA=hA, tt=tt: e.dma_start(
                out=bass.AP(pe_d.tensor, VOF + hA * PED + tt * 128 * PEW,
                            [[PEW, NP], [PED, 2], [1, PEW]]),
                in_=C[:, base + 576:base + 1152].rearrange(
                    "p (a b) -> p a b", a=2))), f"gPe{slot}")
            pe_dma_tok[(pair, tt)] = tok

        pe_build(0)
        pe_build(1)

        def patch_list(m):
            tiles = []
            if m >= 1:
                tiles.append((2 * m - 1, 0, ((512, 64, 0, 0, 16, 0),)))
            tiles.append((2 * m, 0, ((0, 0, 0, 0, 128, 0),
                                     (512, 0, 0, 128, 16, 128))))
            if 2 * m + 1 < NT:
                tiles.append((2 * m + 1, 112,
                              ((512, 32, 1, 112, 16, 112),
                               (256, 0, 1, 128, 128, 128))))
            if 2 * m + 2 < NT:
                tiles.append((2 * m + 2, 240, ((768, 0, 1, 240, 16, 240),)))
            return tiles

        def issue_patch(m):
            slot = m % 2
            q0 = m * 256
            pws = [patch_used.get((pair, m - 2)) or (
                       patch_used.get((pair - 1, 6 + m)) if m < 2 else None),
                   pe_dma_tok.get((pair, 2 * m)),
                   pe_dma_tok.get((pair, 2 * m + 1))]
            tok = None
            for (tb, a0, pieces) in patch_list(m):
                kb = tb * 128
                for (cb, poff, var, qa, Qd, tqo) in pieces:
                    qab = q0 + qa
                    tok = P.d("sync", (lambda e, var=var, Qd=Qd, cb=cb,
                              poff=poff, kb=kb, qab=qab, slot=slot,
                              hA=hA: e.dma_start(
                        out=bass.AP(hC_, poff * 13696 + PAT + slot * 1024 + cb,
                                    [[13696, Qd], [128, 2], [1, 128]]),
                        in_=bass.AP(pe_d.tensor,
                                    var * VOF + hA * PED + qab * PEW
                                    + (kb - qab + 143),
                                    [[PEW - 1, Qd], [PED, 2], [1, 128]]))),
                        f"cPt{slot}", waits=tuple(w for w in pws if w))
            pat_tok[(pair, m)] = tok

        sc_tok = {}
        sc_last = {}
        padd_last = {}
        relu_tok = {}
        sq_tok = {}
        av_tok = {}
        av_done = {}
        kre_tok = {}
        ce_tok = {}
        t1_tok = {}
        rst_tok = {}
        eps_tok = {}
        recip_tok = {}
        nrm_tok = {}
        t2_tok = {}
        spe_tok = {}

        def emit_av(m2, bb):
            g0 = m2 * 32 + bb * 2
            for hh in range(2):
                cbank = ps[3 if hh == 0 else 6]
                w2 = [sq_tok[g0 + hh]]
                if bb == 0:
                    w2 += [ce_tok.get((m2 - 1, hh)) or (prev_tail
                                                        if m2 < 1 else None)]
                    if hh == 0:
                        w2.append(vg_ready)
                vofs = VG + bb * 260 + (2 * (pair % 2) + hh) * 65
                tso = TSL + ((g0 + hh) % 8) * 256
                mmv = P.c("tensor", (lambda e, cbank=cbank, vofs=vofs, hh=hh,
                          bb=bb, tso=tso: e.matmul(
                    cbank[0:65, 0:256],
                    B[:, vofs:vofs + 65],
                    B[:, tso:tso + 256],
                    start=(bb == 0), stop=(bb == 15),
                    skip_group_check=True)),
                    waits=tuple(w for w in w2 if w))
                av_tok[g0 + hh] = mmv
            if bb == 15:
                av_done[m2] = mmv

        def ctx_evacs(m2):
            for hh in range(2):
                cbank = ps[3 if hh == 0 else 6]
                P.c("vector", (lambda e, cbank=cbank, hh=hh: e.tensor_copy(
                    out=C[0:64, CTM + hh * 256:CTM + (hh + 1) * 256],
                    in_=cbank[0:64, 0:256])),
                    waits=(av_done[m2], t1_tok.get(m2 - 1),
                           rst_tok.get(m2 - 1)))
                ce_tok[(m2, hh)] = P.c("vector", (lambda e, cbank=cbank,
                                       hh=hh: e.tensor_copy(
                    out=C[64:65, RSR + hh * 256:RSR + (hh + 1) * 256],
                    in_=cbank[64:65, 0:256])))

        def t1_stage(m2):
            w0 = [ce_tok[(m2, 0)], ce_tok[(m2, 1)],
                  nrm_tok.get(m2 - 1, (None, None))[0],
                  nrm_tok.get(m2 - 1, (None, None))[1],
                  eps_tok.get(m2 - 1)]
            if m2 == 0:
                w0 += [ptok_done, ptok_ev[-2]]
            tp = None
            for c in range(2):
                for hh in range(2):
                    tp = P.c("tensor", (lambda e, c=c, hh=hh: e.transpose(
                        ps[4][:, c * 128 + hh * 64:c * 128 + (hh + 1) * 64],
                        C[0:64, CTM + hh * 256 + c * 128:
                          CTM + hh * 256 + (c + 1) * 128],
                        ids[0:64, 0:64])),
                        waits=tuple(w for w in w0 if w)
                        if (c, hh) == (0, 0) else ())
            t1_tok[m2] = tp
            for c in range(2):
                for hh in range(2):
                    tp = P.c("tensor", (lambda e, c=c, hh=hh: e.transpose(
                        ps[4][:, 256 + c * 2 + hh:256 + c * 2 + hh + 1],
                        C[64:65, RSR + hh * 256 + c * 128:
                          RSR + hh * 256 + (c + 1) * 128],
                        ids[64:65, 64:65])))
            rst_tok[m2] = tp
            rcol = RSB + (m2 % 2) * 8
            eps_tok[m2] = P.c("vector", (lambda e, rcol=rcol: e.tensor_scalar(
                C[:, rcol:rcol + 4], ps[4][:, 256:260], EPS_AT, None,
                ALU.add)), waits=(tp,))
            recip_tok[m2] = P.c("vector", (lambda e, rcol=rcol: e.reciprocal(
                C[:, rcol + 4:rcol + 8], C[:, rcol:rcol + 4])),
                waits=(eps_tok[m2],))

        def norm_stage(m2):
            rcol = RSB + (m2 % 2) * 8
            tk = None
            for c in range(2):
                for hh in range(2):
                    u = c * 2 + hh
                    ws = [recip_tok[m2], t2_tok.get(m2 - 1)] if u == 0 else []
                    tk = P.c("vector", (lambda e, c=c, hh=hh,
                             rcol=rcol, u=u: e.tensor_scalar(
                        C[:, XQ + c * 128 + hh * 64:
                          XQ + c * 128 + (hh + 1) * 64],
                        ps[4][:, c * 128 + hh * 64:
                              c * 128 + (hh + 1) * 64],
                        C[:, rcol + 4 + u:rcol + 5 + u], None, ALU.mult)),
                        waits=tuple(w for w in ws if w))
            nrm_tok[m2] = (tk, tk)

        def t2_stage(m2):
            tp = None
            for c in range(2):
                tp = P.c("tensor", (lambda e, c=c: e.transpose(
                    ps[5][:, c * 128:(c + 1) * 128],
                    C[:, XQ + c * 128:XQ + (c + 1) * 128], ids[:])),
                    waits=(nrm_tok[m2][0], nrm_tok[m2][1],
                           spe_tok.get(m2 - 1),
                           ptok_done if m2 == 0 else None,
                           ptok_ev[-2] if m2 == 0 else None) if c == 0 else ())
            t2_tok[m2] = tp

        def spill_stage(m2):
            cslot = m2 % 2
            e3w = [t2_tok[m2],
                   csp_free.get((pair, m2 - 2)) or (
                       csp_free.get((pair - 1, 6 + m2)) if m2 < 2 else None)]
            spe_tok[m2] = P.c("vector", (lambda e, cslot=cslot: e.tensor_copy(
                out=C[:, CSP + cslot * 256:CSP + (cslot + 1) * 256],
                in_=ps[5][:, 0:256])), waits=tuple(w for w in e3w if w))
            tok = P.d("scalar", (lambda e, cslot=cslot, pair=pair, m2=m2:
                      e.dma_start(
                out=ctx_d[pair * 128:(pair + 1) * 128,
                          m2 * 256:(m2 + 1) * 256],
                in_=C[:, CSP + cslot * 256:CSP + (cslot + 1) * 256])),
                f"gCx{cslot}", waits=(spe_tok[m2],))
            csp_free[(pair, m2)] = tok
            ctx_written.append(tok)

        issue_patch(0)
        for m in range(_KQB):
            q0m = m * 256
            for tb in (2 * m - 1, 2 * m):
                if 1 <= tb < NT:
                    kra = P.c("vector", (lambda e, tb=tb: e.tensor_scalar(
                        B[:, KTMP + tb * 128:KTMP + (tb + 1) * 128],
                        B[:, KTMP + tb * 128:KTMP + (tb + 1) * 128],
                        cc(D03C), None, ALU.add)),
                        waits=(sc_last.get((m - 1, tb)), d03_tok, ksp[0]))
                    krh = P.c("scalar", (lambda e, tb=tb: e.activation(
                        out=Bb[:, KH + tb * 128:KH + (tb + 1) * 128],
                        in_=B[:, KTMP + tb * 128:KTMP + (tb + 1) * 128],
                        func=AF.Identity)),
                        waits=(kra, sc_last.get((m - 1, tb))))
                    krl = P.c("vector", (lambda e, tb=tb: e.tensor_tensor(
                        Bb[:, KL + tb * 128:KL + (tb + 1) * 128],
                        B[:, KTMP + tb * 128:KTMP + (tb + 1) * 128],
                        Bb[:, KH + tb * 128:KH + (tb + 1) * 128],
                        ALU.subtract)), waits=(krh,))
                    kre_tok[tb] = [krh, krl]
            pmap = {}
            for (tb, a0, pieces) in patch_list(m):
                pmap[tb] = pieces
            pt = pat_tok[(pair, m)]
            slot = m % 2
            lastpa = None
            for b in range(NT):
                for hh in range(2):
                    gsi = m * 32 + b * 2 + hh
                    u = gsi % 6
                    bank = ps[u % 3]
                    co = ((u // 3) % 2) * 256
                    mws = [relu_tok.get(gsi - 6), relu_tok.get(gsi - 3)] + list(qk_split)
                    if b in kre_tok:
                        mws += kre_tok[b]
                    mm = None
                    for ti, (qo, ko) in enumerate(
                            ((QH, KH), (QL, KH), (QH, KL))):
                        mm = P.c("tensor", (lambda e, bank=bank, co=co, b=b,
                                 hh=hh, q0m=q0m, qo=qo, ko=ko, ti=ti:
                                 e.matmul(
                            bank[:, co:co + 256],
                            Bb[:, ko + b * 128:ko + (b + 1) * 128][
                                hh * 64:(hh + 1) * 64, :],
                            Bb[:, qo + q0m:qo + q0m + 256][
                                hh * 64:(hh + 1) * 64, :],
                            start=(ti == 0), stop=(ti == 2))),
                            waits=tuple(w for w in mws if w)
                            if ti == 0 else ())
                    sc_tok[gsi] = mm
                    sc_last[(m, b)] = mm
                    rw = mm
                    if b in pmap:
                        for (cb, poff, var, qa, Qd, tqo) in pmap[b]:
                            pcol = PAT + slot * 1024 + cb + hh * 128
                            rw = P.c("tensor", (lambda e, bank=bank, co=co,
                                     tqo=tqo, Qd=Qd, poff=poff, pcol=pcol:
                                     e.matmul(
                                bank[:, co + tqo:co + tqo + Qd],
                                C[poff:poff + Qd, pcol:pcol + 128],
                                ids[poff:poff + Qd, poff:poff + Qd],
                                start=False, stop=True,
                                skip_group_check=True)),
                                waits=(pt,))
                        lastpa = rw
                    tso = TSL + (gsi % 8) * 256
                    rws = [rw, av_tok.get(gsi - 8)]
                    if gsi == 0:
                        rws.append(snb_ev)
                    if hh == 0:
                        rt = P.c("scalar", (lambda e, bank=bank, co=co,
                                 tso=tso: e.activation(
                            out=B[:, tso:tso + 256], in_=bank[:, co:co + 256],
                            func=AF.Relu, bias=cc(SNB), scale=1.0)),
                            waits=tuple(w for w in rws if w))
                        sq = P.c("scalar", (lambda e, tso=tso: e.activation(
                            out=B[:, tso:tso + 256], in_=B[:, tso:tso + 256],
                            func=AF.Square)), waits=(rt,))
                    else:
                        rt = P.c("vector", (lambda e, bank=bank, co=co,
                                 tso=tso: e.tensor_scalar(
                            B[:, tso:tso + 256], bank[:, co:co + 256],
                            cc(SNB), 0.0, ALU.add, ALU.max)),
                            waits=tuple(w for w in rws if w))
                        sq = P.c("vector", (lambda e, tso=tso:
                                 e.scalar_tensor_tensor(
                            B[:, tso:tso + 256], B[:, tso:tso + 256], 0.0,
                            B[:, tso:tso + 256], ALU.add, ALU.mult)),
                            waits=(rt,))
                    relu_tok[gsi] = rt
                    sq_tok[gsi] = sq
                if b >= 2:
                    emit_av(m, b - 2)
                if b == 5 and 2 * m + 2 < NT:
                    pe_build(2 * m + 2)
                elif b == 8 and 2 * m + 3 < NT:
                    pe_build(2 * m + 3)
                elif b == 9 and m + 1 < 8:
                    issue_patch(m + 1)
                if m > 0:
                    if b == 0:
                        emit_av(m - 1, 14)
                    elif b == 1:
                        emit_av(m - 1, 15)
                        ctx_evacs(m - 1)
                    elif b == 4:
                        t1_stage(m - 1)
                    elif b == 7:
                        norm_stage(m - 1)
                    elif b == 10:
                        t2_stage(m - 1)
                    elif b == 12:
                        spill_stage(m - 1)
            patch_used[(pair, m)] = lastpa
            padd_last[m] = lastpa
        if _KQB == 8:
            emit_av(7, 14)
            emit_av(7, 15)
            ctx_evacs(7)
            t1_stage(7)
            norm_stage(7)
            t2_stage(7)
            spill_stage(7)
            prev_tail = spe_tok[7]

    P.barrier()

    if _KPH < 2 or _KNP < 8:
        P.ops["gpsimd"].append((None, tuple(
            (sname, n) for sname, n in P.dma.items()
            if sname.startswith("g")), None))
        P.ops["sync"].append((None, tuple(
            (sname, n) for sname, n in P.dma.items()
            if sname.startswith("c")), None))
        return _finish(nc, P, sems, es)

    ctx_all0 = ("gCx0", P.dma["gCx0"])
    ctx_all1 = ("gCx1", P.dma["gCx1"])
    lastwo = {}
    lastxr = {}
    lastctx = {}
    bankfree2 = {}
    p2ev = []
    ln2_evs = []
    for qb in range(4):
        mmlast = None
        for cchunk in range(NDC):
            slot = cchunk % 2
            cws = [ctx_all0, ctx_all1,
                   lastctx.get((qb, cchunk - 2)) or (
                       lastctx.get((qb - 1, NDC - 2 + cchunk)) if cchunk < 2
                       and qb >= 1 else None)]
            cl = P.d("sync", (lambda e, cchunk=cchunk, qb=qb, slot=slot:
                     e.dma_start(
                out=C[:, CTX2 + slot * 512: CTX2 + (slot + 1) * 512],
                in_=ctx_d[cchunk * 128:(cchunk + 1) * 128,
                          qb * 512:(qb + 1) * 512])), f"cCr{slot}",
                waits=tuple(w for w in cws if w))
            wl = P.d("sync", (lambda e, cchunk=cchunk, slot=slot: e.dma_start(
                out=C[:, WOS + slot * 1024: WOS + (slot + 1) * 1024],
                in_=wot_d[cchunk * 128:(cchunk + 1) * 128, :])), f"cWo{slot}",
                waits=(lastwo.get(slot),))
            cxh, cxl = split("scalar",
                             W[:, 4096 + slot * 1024: 4096 + slot * 1024 + 512],
                             W[:, 4096 + slot * 1024 + 512:
                               4096 + slot * 1024 + 1024],
                             C[:, CTX2 + slot * 512: CTX2 + (slot + 1) * 512],
                             (cl, lastwo.get(slot)))
            woh, wol = split("scalar",
                             W[:, slot * 2048: slot * 2048 + 1024],
                             W[:, slot * 2048 + 1024: slot * 2048 + 2048],
                             C[:, WOS + slot * 1024: WOS + (slot + 1) * 1024],
                             (wl, lastwo.get(slot)))
            mws = [cxh, cxl, woh, wol]
            if qb >= 1 and cchunk == 0:
                mws.extend(p2ev[-2:])
            for dchunk in range(NDC):
                first = True
                for (wofs, xofs) in ((0, 4096), (1024, 4096), (0, 4608)):
                    mm = P.c("tensor", (lambda e, dchunk=dchunk, slot=slot,
                             cchunk=cchunk, wofs=wofs, xofs=xofs, first=first:
                             e.matmul(
                        ps[dchunk][:, 0:512],
                        W[:, slot * 2048 + wofs + dchunk * 128:
                          slot * 2048 + wofs + (dchunk + 1) * 128],
                        W[:, xofs + slot * 1024: xofs + slot * 1024 + 512],
                        start=(cchunk == 0 and first),
                        stop=(cchunk == NDC - 1 and wofs == 0 and xofs == 4608))),
                        waits=tuple(mws) if (dchunk == 0 and first) else ())
                    first = False
                    mmlast = mm
            lastctx[(qb, cchunk)] = mm
            lastwo[slot] = mm
        for dchunk in range(NDC):
            ev = P.c("scalar", (lambda e, dchunk=dchunk: e.activation(
                out=C[:, OFM + dchunk * 512: OFM + (dchunk + 1) * 512],
                in_=ps[dchunk][:, 0:512], func=AF.Identity,
                bias=cst[:, BOB + dchunk:BOB + dchunk + 1], scale=1.0)),
                waits=(mmlast, cC_tok) if dchunk == 0 else ())
            p2ev.append(ev)
        wo_ev_last = p2ev[-1]
        for u in range(4):
            tt = qb * 4 + u
            slot = tt % 2
            xl = P.d("sync", (lambda e, tt=tt, slot=slot: e.dma_start(
                out=C[:, XRE + slot * 1024: XRE + (slot + 1) * 1024],
                in_=x_d[tt * NP:(tt + 1) * NP, :])), f"cXr{slot}",
                waits=(lastxr.get(slot),))
            prev_add = lastxr.get(1 - slot)
            addlast = None
            for dchunk in range(NDC):
                bank = ps[dchunk % 2]
                tp = P.c("tensor", (lambda e, bank=bank, dchunk=dchunk, u=u:
                         e.transpose(
                    bank[:, 128:256],
                    C[:, OFM + dchunk * 512 + u * 128:
                      OFM + dchunk * 512 + u * 128 + 128], ids[:])),
                    waits=((wo_ev_last, prev_add) if prev_add else
                           (wo_ev_last,)) if dchunk == 0 else (
                        (addlast,) if dchunk >= 2 else ()))
                addlast = P.c("vector", (lambda e, bank=bank, tt=tt,
                              dchunk=dchunk, slot=slot: e.tensor_tensor(
                    B[:, tt * 1024 + dchunk * 128:
                      tt * 1024 + (dchunk + 1) * 128],
                    bank[:, 128:256],
                    C[:, XRE + slot * 1024 + dchunk * 128:
                      XRE + slot * 1024 + (dchunk + 1) * 128],
                    ALU.add)), waits=(tp, xl))
            lastxr[slot] = addlast
            x1ap = B[:, tt * 1024:(tt + 1) * 1024]
            r12 = ln_tile(x1ap, RED2, LNS2, XNT2,
                          tuple(ln2_evs[-2:]), (addlast,))
            ln2_evs += xn_transposes(tt, XNT2, r12, bankfree2)
            p2ev.append(ln2_evs[-3])
            p2ev.append(ln2_evs[-1])

    P.barrier()

    acc_tok = None
    lastwf = {}
    for cchunk in range(NDC):
        slot = cchunk % 2
        ldw = P.d("sync", (lambda e, cchunk=cchunk, slot=slot: e.dma_start(
            out=C[:, WFS + slot * 1024: WFS + (slot + 1) * 1024],
            in_=wft_d[cchunk * 128:(cchunk + 1) * 128, :])), f"cWf{slot}",
            waits=(lastwf.get(slot),))
        if cchunk == 0:
            acc_tok = P.c("vector", (lambda e: e.tensor_reduce(
                cc(ACC), C[:, WFS:WFS + 1024], AXX, ALU.add,
                apply_absolute_value=True)), waits=(ldw,))
        else:
            t = P.c("vector", (lambda e, slot=slot: e.tensor_reduce(
                C[:, RED3:RED3 + 1],
                C[:, WFS + slot * 1024: WFS + (slot + 1) * 1024],
                AXX, ALU.add, apply_absolute_value=True)),
                waits=(ldw, acc_tok))
            acc_tok = P.c("vector", (lambda e: e.tensor_add(
                cc(ACC), cc(ACC), C[:, RED3:RED3 + 1])), waits=(t,))
        lastwf[slot] = acc_tok
    tm = P.c("tensor", (lambda e: e.matmul(
        ps[6][0:1, 0:1], cc(ACC), cc(ONEC), start=True, stop=True)),
        waits=(acc_tok, onec_tok))
    sm = P.c("scalar", (lambda e: e.activation(
        out=sc1[0:1, 1:2], in_=ps[6][0:1, 0:1], func=AF.Identity,
        scale=1.0 / (D * D))), waits=(tm,))
    sx = P.c("vector", (lambda e: e.tensor_scalar(
        sc1[0:1, 2:3], sc1[0:1, 1:2], EPS_FF, None, ALU.max)), waits=(sm,))
    tb = P.c("tensor", (lambda e: e.matmul(
        ps[7][:, 0:1], onr[0:1, :], sc1[0:1, 2:3], start=True, stop=True)),
        waits=(sx,))
    sb_tok = P.c("vector", (lambda e: e.tensor_copy(
        out=cc(SSB), in_=ps[7][:, 0:1])), waits=(tb,))
    th_tok = P.c("vector", (lambda e: e.tensor_scalar(
        cc(THR), cc(SSB), 0.5, None, ALU.mult)), waits=(sb_tok,))
    nth_tok = P.c("vector", (lambda e: e.tensor_scalar(
        cc(NTHR), cc(SSB), -0.5, None, ALU.mult)), waits=(sb_tok,))
    tern_last = None
    for cchunk in range(NDC):
        slot = cchunk % 2
        ldw = P.d("sync", (lambda e, cchunk=cchunk, slot=slot: e.dma_start(
            out=C[:, WFS + slot * 1024: WFS + (slot + 1) * 1024],
            in_=wft_d[cchunk * 128:(cchunk + 1) * 128, :])), f"cWf{slot}",
            waits=(lastwf.get(slot),))
        wc = C[:, WFS + slot * 1024: WFS + (slot + 1) * 1024]
        g2 = P.c("vector", (lambda e, wc=wc: e.tensor_scalar(
            C[:, SCR:SCR + 1024], wc, cc(NTHR), None, ALU.is_lt)),
            waits=(nth_tok, th_tok, tern_last, ldw))
        tern_last = P.c("vector", (lambda e, wc=wc, cchunk=cchunk:
                        e.scalar_tensor_tensor(
            W[:, cchunk * 1024:(cchunk + 1) * 1024], wc, cc(THR),
            C[:, SCR:SCR + 1024], ALU.is_gt, ALU.subtract)),
            waits=(g2,))
        lastwf[slot] = tern_last
    tern_done = tern_last

    lastot = {}
    last_add3 = {}
    p3ev = []
    for qb in range(4):
        mmlast = None
        for cchunk in range(NDC):
            mws = [tern_done, p2ev[-1]]
            if qb >= 1 and cchunk == 0:
                mws.append(p3ev[-1])
                mws.append(last_add3.get(0))
                mws.append(last_add3.get(1))
            for dchunk in range(NDC):
                for ti, xf in enumerate((ah, al)):
                    mm = P.c("tensor", (lambda e, dchunk=dchunk, cchunk=cchunk,
                             qb=qb, xf=xf, ti=ti: e.matmul(
                        ps[dchunk][:, 0:512],
                        W[:, cchunk * 1024 + dchunk * 128:
                          cchunk * 1024 + (dchunk + 1) * 128],
                        xf(cchunk, qb * 512, 512),
                        start=(cchunk == 0 and ti == 0),
                        stop=(cchunk == NDC - 1 and ti == 1))),
                        waits=tuple(w for w in mws if w)
                        if (dchunk == 0 and ti == 0) else ())
                    mmlast = mm
        for dchunk in range(NDC):
            ev = P.c("scalar", (lambda e, dchunk=dchunk: e.activation(
                out=C[:, OFM3 + dchunk * 512: OFM3 + (dchunk + 1) * 512],
                in_=ps[dchunk][:, 0:512], func=AF.Identity,
                bias=0.0, scale=cc(SSB))),
                waits=(mmlast, sb_tok) if dchunk == 0 else ())
            p3ev.append(ev)
        ffn_ev_last = p3ev[-1]
        for u in range(4):
            tt = qb * 4 + u
            oslot = tt % 2
            prev_add3 = last_add3.get(1 - (tt % 2))
            addlast = None
            for dchunk in range(NDC):
                bank = ps[dchunk % 2]
                tp = P.c("tensor", (lambda e, bank=bank, dchunk=dchunk, u=u:
                         e.transpose(
                    bank[:, 128:256],
                    C[:, OFM3 + dchunk * 512 + u * 128:
                      OFM3 + dchunk * 512 + u * 128 + 128], ids[:])),
                    waits=((ffn_ev_last, prev_add3) if prev_add3 else
                           (ffn_ev_last,)) if dchunk == 0 else (
                        (addlast,) if dchunk >= 2 else ()))
                aws = [tp]
                if dchunk == 0:
                    aws.append(lastot.get(oslot))
                addlast = P.c("vector", (lambda e, bank=bank, tt=tt,
                              dchunk=dchunk, oslot=oslot: e.tensor_tensor(
                    C[:, OTL + oslot * 1024 + dchunk * 128:
                      OTL + oslot * 1024 + (dchunk + 1) * 128],
                    bank[:, 128:256],
                    B[:, tt * 1024 + dchunk * 128:
                      tt * 1024 + (dchunk + 1) * 128],
                    ALU.add)), waits=tuple(w for w in aws if w))
            ot = P.d("gpsimd", (lambda e, tt=tt, oslot=oslot: e.dma_start(
                out=out_d[tt * NP:(tt + 1) * NP, :],
                in_=C[:, OTL + oslot * 1024: OTL + (oslot + 1) * 1024])),
                f"gOut{oslot}", waits=(addlast,))
            lastot[oslot] = ot
            last_add3[tt % 2] = addlast

    P.ops["gpsimd"].append((None, (("gOut0", P.dma["gOut0"]),
                                   ("gOut1", P.dma["gOut1"])), None))
    return _finish(nc, P, sems, es)


def _finish(nc, P, sems, es):
    def runner(eng_name):
        def body(e):
            maxw = {}
            for fn, waits, inc in P.ops[eng_name]:
                for (s, v) in waits:
                    if maxw.get(s, -1) >= v:
                        continue
                    maxw[s] = v
                    e.wait_ge(sems[s], v)
                if fn is None:
                    continue
                ins = fn(e)
                if inc is not None:
                    ins.then_inc(sems[inc[0]], inc[1])
        return body

    with nc.Block() as block:
        block.sync(runner("sync"))
        block.tensor(runner("tensor"))
        block.vector(runner("vector"))
        block.scalar(runner("scalar"))
        block.gpsimd(runner("gpsimd"))

    es.close()
    return nc


_NC = None


def kernel(L0_attn_iQ, L0_attn_Wa, L0_attn_ba, L0_attn_rel_emb, L0_attn_sn_bias,
           L0_attn_Wo, L0_attn_bo, L0_ffn_weight):
    global _NC
    if _NC is None:
        _NC = build()
    iQ = np.asarray(L0_attn_iQ, dtype=np.float32)
    wat = np.ascontiguousarray(np.asarray(L0_attn_Wa, dtype=np.float32).T)
    ba = np.asarray(L0_attn_ba, dtype=np.float32)
    bat = np.ascontiguousarray(ba.reshape(24, NP).T)
    bavh = np.ascontiguousarray(ba[2048:].reshape(1, D))
    relT0 = np.asarray(L0_attn_rel_emb, dtype=np.float32).T
    relT = np.ascontiguousarray(np.concatenate([relT0, relT0], axis=0))
    snb = np.asarray(L0_attn_sn_bias, dtype=np.float32).reshape(1, 1)
    wot = np.ascontiguousarray(np.asarray(L0_attn_Wo, dtype=np.float32).T)
    bot = np.ascontiguousarray(np.asarray(L0_attn_bo, dtype=np.float32)
                               .reshape(8, NP).T)
    wft = np.ascontiguousarray(np.asarray(L0_ffn_weight, dtype=np.float32).T)
    ident = np.eye(NP, dtype=np.float32)
    onesr = np.ones((1, NP), dtype=np.float32)
    in_maps = []
    for c in range(8):
        in_maps.append({
            "x": np.ascontiguousarray(iQ[c]),
            "wat": wat, "bat": bat, "bavd": bavh, "relT": relT, "snb": snb,
            "wot": wot, "bot": bot, "wft": wft,
            "ident": ident, "onesr": onesr,
        })
    res = run_bass_kernel_spmd(_NC, in_maps, core_ids=list(range(8)))
    out = np.stack([np.asarray(res.results[c]["out"]) for c in range(8)], axis=0)
    return out


# version 16
# speedup vs baseline: 1.0136x; 1.0136x over previous
import contextlib
import os
import numpy as np
import concourse.bass as bass
import concourse.mybir as mybir
import concourse.bass_utils as _bu
from concourse.bass_utils import run_bass_kernel_spmd

if not getattr(_bu, "_ldwopt_patched", False):
    _orig_run_command = _bu.run_command

    def _run_command_ldwopt(cmd, *a, **kw):
        try:
            cmd = ["--enable-ldw-opt=true" if c == "--enable-ldw-opt=false"
                   else c for c in cmd]
        except TypeError:
            pass
        return _orig_run_command(cmd, *a, **kw)

    _bu.run_command = _run_command_ldwopt
    _bu._ldwopt_patched = True

F32 = mybir.dt.float32
BF16 = mybir.dt.bfloat16
AF = mybir.ActivationFunctionType
ALU = mybir.AluOpType
AXX = mybir.AxisListType.X

NP = 128
S = 2048
D = 1024
NT = S // NP
NDC = D // NP
NB = 4
PEW = 288
PED = S * PEW
EPS_LN = 1e-5
EPS_AT = 1e-8
EPS_FF = 1e-5
_KPH = int(os.environ.get("KPH", "3"))
_KNP = int(os.environ.get("KNP", "8"))
_KQB = int(os.environ.get("KQB", "8"))


class Plan:
    ES = ("sync", "tensor", "vector", "scalar", "gpsimd")

    def __init__(self):
        self.ops = {e: [] for e in self.ES}
        self.pos = {e: 0 for e in self.ES}
        self.dma = {}
        self.pending = {e: () for e in self.ES}

    def c(self, eng, fn, waits=()):
        w = tuple(self.pending[eng]) + tuple(t for t in waits if t is not None)
        self.pending[eng] = ()
        self.ops[eng].append((fn, w, ("@" + eng, 1)))
        self.pos[eng] += 1
        return ("@" + eng, self.pos[eng])

    def d(self, eng, fn, sem, waits=()):
        w = tuple(self.pending[eng]) + tuple(t for t in waits if t is not None)
        if eng == "gpsimd" and self.dma.get(sem, 0) > 0:
            w = w + ((sem, self.dma[sem]),)
        self.pending[eng] = ()
        self.dma[sem] = self.dma.get(sem, 0) + 16
        self.ops[eng].append((fn, w, (sem, 16)))
        return (sem, self.dma[sem])

    def at(self, eng):
        return ("@" + eng, self.pos[eng])

    def barrier(self):
        snap = tuple(("@" + e, self.pos[e]) for e in ("tensor", "vector", "scalar")
                     if self.pos[e] > 0)
        for e in self.ES:
            self.pending[e] = snap


def cpy(e, out, in_):
    if hasattr(e, "tensor_copy"):
        return e.tensor_copy(out=out, in_=in_)
    return e.copy(out=out, in_=in_)


def evac_segments(i):
    q0 = 128 * i
    wlo = max(q0 - 16, 0)
    whi = min(q0 + 144, S)
    blocks = []
    for b in range(NB):
        s0, s1 = 512 * b, 512 * b + 512
        segs = []
        if min(s1, wlo) > s0:
            segs.append((s0, min(s1, wlo), "lo"))
        if min(s1, whi) > max(s0, wlo):
            segs.append((max(s0, wlo), min(s1, whi), "win"))
        if s1 > max(s0, whi):
            segs.append((max(s0, whi), s1, "hi"))
        blocks.append(segs)
    return wlo, whi, blocks


def build():
    nc = bass.Bass("TRN2", target_bir_lowering=False)

    x_d = nc.dram_tensor("x", [S, D], F32, kind="ExternalInput").ap()
    wat_d = nc.dram_tensor("wat", [D, 3 * D], F32, kind="ExternalInput").ap()
    bat_d = nc.dram_tensor("bat", [NP, 24], F32, kind="ExternalInput").ap()
    relT_d = nc.dram_tensor("relT", [NP, 33], F32, kind="ExternalInput").ap()
    snb_d = nc.dram_tensor("snb", [1, 1], F32, kind="ExternalInput").ap()
    wot_d = nc.dram_tensor("wot", [D, D], F32, kind="ExternalInput").ap()
    bot_d = nc.dram_tensor("bot", [NP, 8], F32, kind="ExternalInput").ap()
    bav_d = nc.dram_tensor("bavd", [1, D], F32, kind="ExternalInput").ap()
    wft_d = nc.dram_tensor("wft", [D, D], F32, kind="ExternalInput").ap()
    ident_d = nc.dram_tensor("ident", [NP, NP], F32, kind="ExternalInput").ap()
    onesr_d = nc.dram_tensor("onesr", [1, NP], F32, kind="ExternalInput").ap()
    out_d = nc.dram_tensor("out", [S, D], F32, kind="ExternalOutput").ap()
    pe_d = nc.dram_tensor("pedram", [32 * PED], F32, kind="Internal").ap()
    ctx_d = nc.dram_tensor("ctxd", [D, S], F32, kind="Internal").ap()

    es = contextlib.ExitStack()
    Ah = es.enter_context(nc.sbuf_tensor("arnAh", [NP, 16384], BF16))
    Al = es.enter_context(nc.sbuf_tensor("arnAl", [NP, 16384], BF16))
    B = es.enter_context(nc.sbuf_tensor("arnB", [NP, 16384], F32))
    C = es.enter_context(nc.sbuf_tensor("arnC", [NP, 13696], F32))
    W = es.enter_context(nc.sbuf_tensor("wspl", [NP, 8192], BF16))
    ids = es.enter_context(nc.sbuf_tensor("ids", [NP, NP], F32))
    cst = es.enter_context(nc.sbuf_tensor("cst", [NP, 64], F32))
    sc1 = es.enter_context(nc.sbuf_tensor("sc1", [1, 8], F32))
    onr = es.enter_context(nc.sbuf_tensor("onr", [1, NP], F32))
    bav = es.enter_context(nc.sbuf_tensor("bav", [1, D], F32))
    rlt = es.enter_context(nc.sbuf_tensor("rlt", [NP, 33], F32))
    ps = [es.enter_context(nc.psum_tensor(f"ps{i}", [NP, 512], F32))
          for i in range(8)]
    semnames = ["@tensor", "@vector", "@scalar",
                "cX0", "cX1", "cX2", "cX3", "cC", "cWq", "cWk", "cWv",
                "cPt0", "cPt1", "cCr0", "cCr1", "cXr0", "cXr1",
                "cWo0", "cWo1", "cWf0", "cWf1",
                "gPe0", "gPe1", "gCx0", "gCx1", "gOut0", "gOut1"]
    sems = {n: es.enter_context(nc.semaphore(n.replace("@", "e_")))
            for n in semnames}

    hAh = Ah[0:1, 0:1].tensor
    hAl = Al[0:1, 0:1].tensor
    hC_ = C[0:1, 0:1].tensor
    hB_ = B[0:1, 0:1].tensor
    VOF = 16 * PED
    Bb = B[:, 12288:16384].bitcast(BF16)

    P = Plan()

    QTMP, KTMP, VG, TSL = 0, 2048, 4096, 8256
    QH, QL, KH, KL = 0, 2048, 4096, 6144
    XT, LNS, XNT, RED = 0, 2048, 3072, 4096
    PTK, WST, PET, PAT = 4200, 5256, 7304, 9608
    NPT, CTM, RSR, XQ, CSP, RSB = 11656, 11784, 12296, 12808, 13064, 13576
    OFM, WOS, CTX2, XRE, LNS2, XNT2, RED2 = 0, 4096, 6144, 7168, 9216, 10240, 11264
    WFS, SCR, OFM3, OTL, RED3 = 0, 2048, 3072, 7168, 9216
    WQH, WQL, WKH, WKL, WVH, WVL = 0, 1024, 2048, 3072, 4096, 6144
    BAB, BOB, ONEC, SNB, BQS, SSB, THR, NTHR, ACC, EPSC, D03C = (
        0, 24, 32, 33, 34, 35, 36, 37, 38, 39, 40)

    def cc(col, n=1):
        return cst[:, col:col + n]

    def split(hi_eng, hi_ap, lo_ap, src_ap, waits):
        if hi_eng == "scalar":
            ht = P.c("scalar", (lambda e, hi_ap=hi_ap, src_ap=src_ap:
                     e.activation(out=hi_ap, in_=src_ap, func=AF.Identity)),
                     waits=waits)
        else:
            ht = P.c("vector", (lambda e, hi_ap=hi_ap, src_ap=src_ap:
                     e.tensor_copy(out=hi_ap, in_=src_ap)), waits=waits)
        lt = P.c("vector", (lambda e, lo_ap=lo_ap, src_ap=src_ap, hi_ap=hi_ap:
                 e.tensor_tensor(lo_ap, src_ap, hi_ap, ALU.subtract)),
                 waits=(ht,))
        return ht, lt

    def ln_tile(x_ap, red, lns, xnt, extra_r3_waits, r1_waits):
        r1 = P.c("vector", (lambda e: e.tensor_reduce(
            C[:, red:red + 1], x_ap, AXX, ALU.add)), waits=r1_waits)
        r2 = P.c("scalar", (lambda e: e.activation(
            out=C[:, red + 1:red + 2], in_=C[:, red:red + 1],
            func=AF.Identity, scale=-1.0 / D)), waits=(r1,))
        r3 = P.c("vector", (lambda e: e.tensor_scalar(
            C[:, xnt:xnt + 1024], x_ap, C[:, red + 1:red + 2], None, ALU.add)),
            waits=(r2,) + tuple(extra_r3_waits))
        r4 = P.c("scalar", (lambda e: e.activation(
            out=C[:, lns:lns + 1024], in_=C[:, xnt:xnt + 1024], func=AF.Square,
            accum_out=C[:, red + 2:red + 3])), waits=(r3,))
        r5 = P.c("scalar", (lambda e: e.activation(
            out=C[:, red + 3:red + 4], in_=C[:, red + 2:red + 3],
            func=AF.Identity, scale=1.0 / D, bias=cc(EPSC))),
            waits=(r4, epsc_tok))
        r6 = P.c("scalar", (lambda e: e.activation(
            out=C[:, red + 4:red + 5], in_=C[:, red + 3:red + 4],
            func=AF.Sqrt)), waits=(r5,))
        r7 = P.c("vector", (lambda e: e.reciprocal(
            C[:, red + 5:red + 6], C[:, red + 4:red + 5])), waits=(r6,))
        r8 = P.c("vector", (lambda e: e.tensor_mul(
            C[:, red + 6:red + 7], C[:, red + 3:red + 4], C[:, red + 5:red + 6])),
            waits=(r7,))
        r9 = P.c("vector", (lambda e: e.tensor_mul(
            C[:, red + 7:red + 8], C[:, red + 6:red + 7], C[:, red + 5:red + 6])),
            waits=(r8,))
        r10 = P.c("vector", (lambda e: e.tensor_scalar(
            C[:, red + 8:red + 9], C[:, red + 7:red + 8], -0.5, 1.5,
            ALU.mult, ALU.add)), waits=(r9,))
        r11 = P.c("vector", (lambda e: e.tensor_mul(
            C[:, red + 9:red + 10], C[:, red + 8:red + 9], C[:, red + 5:red + 6])),
            waits=(r10,))
        r12 = P.c("vector", (lambda e: e.tensor_scalar(
            C[:, xnt:xnt + 1024], C[:, xnt:xnt + 1024],
            C[:, red + 9:red + 10], None, ALU.mult)), waits=(r11, r4))
        return r12

    def xn_transposes(tt, xnt, r12, bankfree):
        toks = []
        for gg in range(2):
            bank = ps[4 + gg]
            tws = [r12]
            if (4 + gg) in bankfree:
                tws.append(bankfree[4 + gg])
            tp = None
            for jj in range(4):
                dc = gg * 4 + jj
                tp = P.c("tensor", (lambda e, bank=bank, jj=jj, dc=dc, xnt=xnt:
                         e.transpose(
                    bank[:, jj * 128:(jj + 1) * 128],
                    C[:, xnt + dc * 128: xnt + (dc + 1) * 128], ids[:])),
                    waits=tuple(tws) if jj == 0 else ())
            dims = [[16384, NP], [2048, 4], [1, 128]]
            off = gg * 4 * 2048 + tt * 128
            hi_ap = bass.AP(hAh, off, dims)
            lo_ap = bass.AP(hAl, off, dims)
            src = bass.AP(bank[0:1, 0:1].tensor, 0, [[512, NP], [128, 4], [1, 128]])
            ht, lt = split("scalar" if gg == 0 else "vector",
                           hi_ap, lo_ap, src, (tp,))
            bankfree[4 + gg] = lt
            toks += [ht, lt]
        return toks

    P.d("sync", lambda e: e.dma_start(out=ids[:], in_=ident_d), "cC")
    P.d("sync", lambda e: e.dma_start(out=onr[:], in_=onesr_d), "cC")
    P.d("sync", lambda e: e.dma_start(out=rlt[:], in_=relT_d), "cC")
    P.d("sync", lambda e: e.dma_start(out=sc1[0:1, 0:1], in_=snb_d), "cC")
    P.d("sync", lambda e: e.dma_start(out=cc(BAB, 24), in_=bat_d), "cC")
    P.d("sync", lambda e: e.dma_start(out=cc(BOB, 8), in_=bot_d), "cC")
    P.d("sync", lambda e: e.dma_start(out=bav[0:1, :], in_=bav_d), "cC")
    cC_tok = ("cC", P.dma["cC"])

    onec_tok = P.c("vector", lambda e: e.memset(cc(ONEC), 1.0))
    epsc_tok = P.c("vector", lambda e: e.memset(cc(EPSC), EPS_LN))
    pet_ms = P.c("vector", lambda e: e.memset(C[:, PET:PET + 2304], 0.0))
    d03_tok = P.c("vector", lambda e: e.tensor_tensor(
        cc(D03C), rlt[:, 0:1], rlt[:, 32:33], ALU.subtract), waits=(cC_tok,))
    vone_tok = P.c("vector", lambda e: e.memset(
        bass.AP(hB_, VG + 64, [[16384, NP], [260, 16], [65, 4]]), 1.0))
    t_snb = P.c("tensor", lambda e: e.matmul(
        ps[7][:, 0:1], onr[0:1, :], sc1[0:1, 0:1], start=True, stop=True),
        waits=(cC_tok,))
    snb_ev = P.c("vector", lambda e: e.tensor_copy(out=cc(SNB), in_=ps[7][:, 0:1]),
                 waits=(t_snb,))

    bankfree = {}
    lastxc = {}
    for tt in range(NT):
        slot = tt % 4
        ld = P.d("sync", (lambda e, tt=tt, slot=slot: e.dma_start(
            out=B[:, slot * 1024:(slot + 1) * 1024],
            in_=x_d[tt * NP:(tt + 1) * NP, :])), f"cX{slot}",
            waits=(lastxc.get(slot),))
        x_ap = B[:, slot * 1024:(slot + 1) * 1024]
        prevT = (P.at("tensor"),) if tt > 0 else ()
        r12 = ln_tile(x_ap, RED, LNS, XNT, prevT, (ld,))
        lastxc[slot] = r12
        xn_transposes(tt, XNT, r12, bankfree)

    P.barrier()

    def ah(dc, c0, n):
        return Ah[:, dc * 2048 + c0: dc * 2048 + c0 + n]

    def al(dc, c0, n):
        return Al[:, dc * 2048 + c0: dc * 2048 + c0 + n]

    pe_dma_tok = {}
    pat_tok = {}
    patch_used = {}
    csp_free = {}
    vg_ready = None
    prev_tail = None
    ctx_written = []
    pre_w = {}

    def load_pair_w(pair, wqkf, wvf, stg_free=()):
        g = pair // 2
        wq_ld = P.d("sync", (lambda e, pair=pair: e.dma_start(
            out=C[:, WST:WST + 1024].rearrange("p (a b) -> p a b", a=8),
            in_=bass.AP(wat_d.tensor, pair * 128,
                        [[3 * D, NP], [3 * D * NP, 8], [1, 128]]))), "cWq",
            waits=(wqkf,) + tuple(stg_free))
        wk_ld = P.d("sync", (lambda e, pair=pair: e.dma_start(
            out=C[:, WST + 1024:WST + 2048].rearrange("p (a b) -> p a b", a=8),
            in_=bass.AP(wat_d.tensor, D + pair * 128,
                        [[3 * D, NP], [3 * D * NP, 8], [1, 128]]))), "cWk",
            waits=(wqkf,) + tuple(stg_free))
        qh_, ql_ = split("scalar", W[:, WQH:WQH + 1024], W[:, WQL:WQL + 1024],
                         C[:, WST:WST + 1024], (wq_ld, wqkf))
        kh_, kl_ = split("scalar", W[:, WKH:WKH + 1024], W[:, WKL:WKL + 1024],
                         C[:, WST + 1024:WST + 2048], (wk_ld, wqkf))
        vh_ = vl_ = None
        if pair % 2 == 0:
            wv_ld = P.d("sync", (lambda e, g=g: e.dma_start(
                out=C[:, WST:WST + 2048].rearrange("p (a b) -> p a b", a=8),
                in_=bass.AP(wat_d.tensor, 2 * D + g * 256,
                            [[3 * D, NP], [3 * D * NP, 8], [1, 256]]))), "cWv",
                waits=(ql_, kl_, qh_, kh_, wvf))
            vh_, vl_ = split("scalar", W[:, WVH:WVH + 2048], W[:, WVL:WVL + 2048],
                             C[:, WST:WST + 2048], (wv_ld, wvf))
        return (qh_, ql_, kh_, kl_, vh_, vl_)

    prev_qdone = None
    pending_tail = [None, None]
    for pair in range(8 if _KPH >= 1 else 0):
        if pair >= _KNP:
            break
        hA = 2 * pair
        g = pair // 2

        if pair == 0:
            pre_w[0] = load_pair_w(0, None, None)
        wqh_t, wql_t, wkh_t, wkl_t, wvh_t, wvl_t = pre_w[pair]
        if wvh_t is None:
            wvh_t, wvl_t = pre_w[pair - 1][4], pre_w[pair - 1][5]

        bq_tok = P.c("vector", (lambda e, pair=pair: e.tensor_scalar(
            cc(BQS), cst[:, BAB + pair:BAB + pair + 1], 0.125, None, ALU.mult)),
            waits=(cC_tok, prev_qdone))

        qk_evacs = []
        qkn = 0
        for which, whofs, wlofs, wsp in (
                ("q", WQH, WQL, (wqh_t, wql_t)),
                ("k", WKH, WKL, (wkh_t, wkl_t))):
            for qb in range(4):
                if pending_tail[0] and qkn >= 1:
                    pending_tail[0].pop(0)()
                    if not pending_tail[0]:
                        prev_tail = pending_tail[1]()
                        pending_tail[0] = None
                qkn += 1
                bank = ps[qb % 2]
                mws = list(wsp)
                if len(qk_evacs) >= 2:
                    mws.append(qk_evacs[-2])
                mm = None
                first = True
                for dc in range(NDC):
                    for (wo, xf) in ((whofs, ah), (wlofs, ah), (whofs, al)):
                        last = (dc == NDC - 1 and wo == whofs and xf is al)
                        mm = P.c("tensor", (lambda e, bank=bank, wo=wo, dc=dc,
                                            qb=qb, xf=xf, first=first, last=last:
                                 e.matmul(
                            bank[:, 0:512],
                            W[:, wo + dc * 128: wo + (dc + 1) * 128],
                            xf(dc, qb * 512, 512),
                            start=first, stop=last)),
                            waits=tuple(mws) if first else ())
                        first = False
                if which == "q":
                    ev = P.c("scalar", (lambda e, bank=bank, qb=qb:
                             e.activation(
                        out=B[:, QTMP + qb * 512: QTMP + qb * 512 + 512],
                        in_=bank[:, 0:512], func=AF.Identity,
                        bias=cc(BQS), scale=0.125)), waits=(mm, bq_tok))
                else:
                    ev = P.c("scalar", (lambda e, bank=bank, qb=qb, pair=pair:
                             e.activation(
                        out=B[:, KTMP + qb * 512: KTMP + qb * 512 + 512],
                        in_=bank[:, 0:512], func=AF.Identity,
                        bias=cst[:, BAB + 8 + pair:BAB + 9 + pair], scale=1.0)),
                        waits=(mm,))
                qk_evacs.append(ev)
        q_done = qk_evacs[3]
        k_done = qk_evacs[7]
        prev_qdone = q_done
        kb0 = P.c("vector", (lambda e: e.tensor_scalar(
            B[:, KTMP:KTMP + 128], B[:, KTMP:KTMP + 128],
            rlt[:, 0:1], None, ALU.add)), waits=(k_done, cC_tok))
        kb1 = P.c("vector", (lambda e: e.tensor_scalar(
            B[:, KTMP + 128:KTMP + 2048], B[:, KTMP + 128:KTMP + 2048],
            rlt[:, 32:33], None, ALU.add)))
        qsp = split("scalar", Bb[:, QH:QH + 2048], Bb[:, QL:QL + 2048],
                    B[:, QTMP:QTMP + 2048], (q_done,))
        ksp = split("scalar", Bb[:, KH:KH + 2048], Bb[:, KL:KL + 2048],
                    B[:, KTMP:KTMP + 2048], (kb0, kb1))
        qk_split = (qsp[0], qsp[1], ksp[0], ksp[1])

        if pair % 2 == 0:
            vprev = None
            vbank = {}
            for tt in range(NT):
                bank = ps[6 + tt % 2]
                mws = [wvh_t, wvl_t, cC_tok, onec_tok, vbank.get(tt % 2)]
                first = True
                for dc in range(NDC):
                    for (wo, xf) in ((WVH, ah), (WVH, al), (WVL, ah)):
                        P.c("tensor", (lambda e, bank=bank, dc=dc, tt=tt,
                                       wo=wo, xf=xf, first=first: e.matmul(
                            bank[:, 0:256],
                            xf(dc, tt * 128, 128),
                            W[:, wo + dc * 256: wo + (dc + 1) * 256],
                            start=first, stop=False)),
                            waits=tuple(mws) if first else ())
                        first = False
                mm = P.c("tensor", (lambda e, bank=bank, g=g: e.matmul(
                    bank[:, 0:256], onr[0:1, :],
                    bav[0:1, g * 256:(g + 1) * 256],
                    start=False, stop=True)))
                vprev = P.c("vector", (lambda e, bank=bank, tt=tt: e.tensor_copy(
                    out=bass.AP(hB_, VG + tt * 260,
                                [[16384, NP], [65, 4], [1, 64]]),
                    in_=bank[:, 0:256])), waits=(mm,))
                vbank[tt % 2] = vprev
            vg_ready = vprev
        v_use_done = P.at("tensor")

        if pair < _KNP - 1 and pair < 7:
            nwv = v_use_done if (pair + 1) % 2 == 0 else None
            pre_w[pair + 1] = load_pair_w(
                pair + 1, v_use_done, nwv,
                stg_free=tuple(t for t in pre_w[pair] if t is not None))

        ptprev = {}
        ptok_ev = []
        for tt in range(NT):
            for hh in range(2):
                bank = 4 + (tt * 2 + hh) % 2
                mws = [q_done, cC_tok, ptprev.get(bank), prev_tail]
                mm = P.c("tensor", (lambda e, bank=bank, hh=hh, tt=tt: e.matmul(
                    ps[bank][:, 0:33],
                    B[:, QTMP + tt * 128: QTMP + (tt + 1) * 128][
                        hh * 64:(hh + 1) * 64, :],
                    rlt[hh * 64:(hh + 1) * 64, :],
                    start=True, stop=True)), waits=tuple(mws))
                peng = "vector" if (tt * 2 + hh) % 2 == 0 else "scalar"
                ev = P.c(peng, (lambda e, bank=bank, tt=tt, hh=hh: cpy(
                    e, C[:, PTK + (tt * 2 + hh) * 33:
                         PTK + (tt * 2 + hh) * 33 + 33],
                    ps[bank][:, 0:33])), waits=(mm,))
                ptprev[bank] = ev
                ptok_ev.append(ev)
        ptok_done = ptok_ev[-1]
        nx1 = P.c("vector", (lambda e: e.tensor_scalar(
            C[:, NPT:NPT + 32],
            bass.AP(hC_, PTK, [[13696, NP], [33, 32]]),
            -1.0, None, ALU.mult)),
            waits=(ptok_done, ptok_ev[-2]))
        nx2 = P.c("vector", (lambda e: e.tensor_scalar(
            C[:, NPT + 32:NPT + 64],
            bass.AP(hC_, PTK + 32, [[13696, NP], [33, 32]]),
            -1.0, None, ALU.mult)),
            waits=(ptok_done, ptok_ev[-2]))
        nx3 = P.c("vector", (lambda e: e.tensor_tensor(
            C[:, NPT + 64:NPT + 96],
            bass.AP(hC_, PTK + 32, [[13696, NP], [33, 32]]),
            bass.AP(hC_, PTK, [[13696, NP], [33, 32]]),
            ALU.subtract)), waits=(ptok_done, ptok_ev[-2]))
        npx_tok = P.c("vector", (lambda e: e.tensor_scalar(
            C[:, NPT + 96:NPT + 128],
            C[:, NPT + 64:NPT + 96], -1.0, None, ALU.mult)),
            waits=(nx1, nx2, nx3))

        def pe_build(tt):
            slot = tt % 2
            base = PET + slot * 1152
            prev = pe_dma_tok.get((pair, tt - 2)) or (
                pe_dma_tok.get((pair - 1, NT - 2 + tt)) if tt < 2 else None)
            bt = None
            for hh in range(2):
                idx = tt * 2 + hh
                pcol = PTK + idx * 33
                n0c = C[:, NPT + idx:NPT + idx + 1]
                n32c = C[:, NPT + 32 + idx:NPT + 33 + idx]
                d32c = C[:, NPT + 64 + idx:NPT + 65 + idx]
                nd32c = C[:, NPT + 96 + idx:NPT + 97 + idx]
                b0 = base + hh * 288
                b2 = base + 576 + hh * 288
                bws = ((npx_tok, pet_ms, prev, ptok_ev[tt * 2],
                        ptok_ev[tt * 2 + 1]) if hh == 0 else ())
                bt = P.c("scalar", (lambda e, b0=b0, pcol=pcol, n0c=n0c:
                         e.activation(
                    out=C[:, b0 + 127:b0 + 160], in_=C[:, pcol:pcol + 33],
                    func=AF.Identity, bias=n0c, scale=1.0)),
                    waits=tuple(w for w in bws if w))
                bt = P.c("scalar", (lambda e, b0=b0, d32c=d32c: e.activation(
                    out=C[:, b0 + 160:b0 + 288], in_=C[:, b0 + 160:b0 + 288],
                    func=AF.Identity, bias=d32c, scale=0.0)))
                bt = P.c("scalar", (lambda e, b2=b2, nd32c=nd32c: e.activation(
                    out=C[:, b2:b2 + 127], in_=C[:, b2:b2 + 127],
                    func=AF.Identity, bias=nd32c, scale=0.0)))
                bt = P.c("scalar", (lambda e, b2=b2, pcol=pcol, n32c=n32c:
                         e.activation(
                    out=C[:, b2 + 127:b2 + 160], in_=C[:, pcol:pcol + 33],
                    func=AF.Identity, bias=n32c, scale=1.0)))
            P.d("gpsimd", (lambda e, base=base, hA=hA, tt=tt: e.dma_start(
                out=bass.AP(pe_d.tensor, hA * PED + tt * 128 * PEW,
                            [[PEW, NP], [PED, 2], [1, PEW]]),
                in_=C[:, base:base + 576].rearrange("p (a b) -> p a b", a=2))),
                f"gPe{slot}", waits=(bt,))
            tok = P.d("gpsimd", (lambda e, base=base, hA=hA, tt=tt: e.dma_start(
                out=bass.AP(pe_d.tensor, VOF + hA * PED + tt * 128 * PEW,
                            [[PEW, NP], [PED, 2], [1, PEW]]),
                in_=C[:, base + 576:base + 1152].rearrange(
                    "p (a b) -> p a b", a=2))), f"gPe{slot}")
            pe_dma_tok[(pair, tt)] = tok

        pe_build(0)
        pe_build(1)

        def patch_list(m):
            tiles = []
            if m >= 1:
                tiles.append((2 * m - 1, 0, ((512, 64, 0, 0, 16, 0),)))
            tiles.append((2 * m, 0, ((0, 0, 0, 0, 128, 0),
                                     (512, 0, 0, 128, 16, 128))))
            if 2 * m + 1 < NT:
                tiles.append((2 * m + 1, 112,
                              ((512, 32, 1, 112, 16, 112),
                               (256, 0, 1, 128, 128, 128))))
            if 2 * m + 2 < NT:
                tiles.append((2 * m + 2, 240, ((768, 0, 1, 240, 16, 240),)))
            return tiles

        def issue_patch(m):
            slot = m % 2
            q0 = m * 256
            pws = [patch_used.get((pair, m - 2)) or (
                       patch_used.get((pair - 1, 6 + m)) if m < 2 else None),
                   pe_dma_tok.get((pair, 2 * m)),
                   pe_dma_tok.get((pair, 2 * m + 1))]
            tok = None
            for (tb, a0, pieces) in patch_list(m):
                kb = tb * 128
                for (cb, poff, var, qa, Qd, tqo) in pieces:
                    qab = q0 + qa
                    tok = P.d("sync", (lambda e, var=var, Qd=Qd, cb=cb,
                              poff=poff, kb=kb, qab=qab, slot=slot,
                              hA=hA: e.dma_start(
                        out=bass.AP(hC_, poff * 13696 + PAT + slot * 1024 + cb,
                                    [[13696, Qd], [128, 2], [1, 128]]),
                        in_=bass.AP(pe_d.tensor,
                                    var * VOF + hA * PED + qab * PEW
                                    + (kb - qab + 143),
                                    [[PEW - 1, Qd], [PED, 2], [1, 128]]))),
                        f"cPt{slot}", waits=tuple(w for w in pws if w))
            pat_tok[(pair, m)] = tok

        sc_tok = {}
        sc_last = {}
        padd_last = {}
        relu_tok = {}
        sq_tok = {}
        av_tok = {}
        av_done = {}
        kre_tok = {}
        ce_tok = {}
        t1_tok = {}
        rst_tok = {}
        eps_tok = {}
        recip_tok = {}
        nrm_tok = {}
        t2_tok = {}
        spe_tok = {}

        def emit_av(m2, bb):
            g0 = m2 * 32 + bb * 2
            for hh in range(2):
                cbank = ps[3 if hh == 0 else 6]
                w2 = [sq_tok[g0 + hh]]
                if bb == 0:
                    w2 += [ce_tok.get((m2 - 1, hh)) or (prev_tail
                                                        if m2 < 1 else None)]
                    if hh == 0:
                        w2.append(vg_ready)
                vofs = VG + bb * 260 + (2 * (pair % 2) + hh) * 65
                tso = TSL + ((g0 + hh) % 8) * 256
                mmv = P.c("tensor", (lambda e, cbank=cbank, vofs=vofs, hh=hh,
                          bb=bb, tso=tso: e.matmul(
                    cbank[0:65, 0:256],
                    B[:, vofs:vofs + 65],
                    B[:, tso:tso + 256],
                    start=(bb == 0), stop=(bb == 15),
                    skip_group_check=True)),
                    waits=tuple(w for w in w2 if w))
                av_tok[g0 + hh] = mmv
            if bb == 15:
                av_done[m2] = mmv

        def ctx_evacs(m2):
            for hh in range(2):
                cbank = ps[3 if hh == 0 else 6]
                P.c("vector", (lambda e, cbank=cbank, hh=hh: e.tensor_copy(
                    out=C[0:64, CTM + hh * 256:CTM + (hh + 1) * 256],
                    in_=cbank[0:64, 0:256])),
                    waits=(av_done[m2], t1_tok.get(m2 - 1),
                           rst_tok.get(m2 - 1)))
                ce_tok[(m2, hh)] = P.c("vector", (lambda e, cbank=cbank,
                                       hh=hh: e.tensor_copy(
                    out=C[64:65, RSR + hh * 256:RSR + (hh + 1) * 256],
                    in_=cbank[64:65, 0:256])))

        def t1_stage(m2):
            w0 = [ce_tok[(m2, 0)], ce_tok[(m2, 1)],
                  nrm_tok.get(m2 - 1, (None, None))[0],
                  nrm_tok.get(m2 - 1, (None, None))[1],
                  eps_tok.get(m2 - 1)]
            if m2 == 0:
                w0 += [ptok_done, ptok_ev[-2]]
            tp = None
            for c in range(2):
                for hh in range(2):
                    tp = P.c("tensor", (lambda e, c=c, hh=hh: e.transpose(
                        ps[4][:, c * 128 + hh * 64:c * 128 + (hh + 1) * 64],
                        C[0:64, CTM + hh * 256 + c * 128:
                          CTM + hh * 256 + (c + 1) * 128],
                        ids[0:64, 0:64])),
                        waits=tuple(w for w in w0 if w)
                        if (c, hh) == (0, 0) else ())
            t1_tok[m2] = tp
            for c in range(2):
                for hh in range(2):
                    tp = P.c("tensor", (lambda e, c=c, hh=hh: e.transpose(
                        ps[4][:, 256 + c * 2 + hh:256 + c * 2 + hh + 1],
                        C[64:65, RSR + hh * 256 + c * 128:
                          RSR + hh * 256 + (c + 1) * 128],
                        ids[64:65, 64:65])))
            rst_tok[m2] = tp
            rcol = RSB + (m2 % 2) * 8
            eps_tok[m2] = P.c("vector", (lambda e, rcol=rcol: e.tensor_scalar(
                C[:, rcol:rcol + 4], ps[4][:, 256:260], EPS_AT, None,
                ALU.add)), waits=(tp,))
            recip_tok[m2] = P.c("vector", (lambda e, rcol=rcol: e.reciprocal(
                C[:, rcol + 4:rcol + 8], C[:, rcol:rcol + 4])),
                waits=(eps_tok[m2],))

        def norm_stage(m2):
            rcol = RSB + (m2 % 2) * 8
            tk = None
            for c in range(2):
                for hh in range(2):
                    u = c * 2 + hh
                    ws = [recip_tok[m2], t2_tok.get(m2 - 1)] if u == 0 else []
                    tk = P.c("vector", (lambda e, c=c, hh=hh,
                             rcol=rcol, u=u: e.tensor_scalar(
                        C[:, XQ + c * 128 + hh * 64:
                          XQ + c * 128 + (hh + 1) * 64],
                        ps[4][:, c * 128 + hh * 64:
                              c * 128 + (hh + 1) * 64],
                        C[:, rcol + 4 + u:rcol + 5 + u], None, ALU.mult)),
                        waits=tuple(w for w in ws if w))
            nrm_tok[m2] = (tk, tk)

        def t2_stage(m2):
            tp = None
            for c in range(2):
                tp = P.c("tensor", (lambda e, c=c: e.transpose(
                    ps[5][:, c * 128:(c + 1) * 128],
                    C[:, XQ + c * 128:XQ + (c + 1) * 128], ids[:])),
                    waits=(nrm_tok[m2][0], nrm_tok[m2][1],
                           spe_tok.get(m2 - 1),
                           ptok_done if m2 == 0 else None,
                           ptok_ev[-2] if m2 == 0 else None) if c == 0 else ())
            t2_tok[m2] = tp

        def spill_stage(m2, pair=pair):
            cslot = m2 % 2
            e3w = [t2_tok[m2],
                   csp_free.get((pair, m2 - 2)) or (
                       csp_free.get((pair - 1, 6 + m2)) if m2 < 2 else None)]
            spe_tok[m2] = P.c("vector", (lambda e, cslot=cslot: e.tensor_copy(
                out=C[:, CSP + cslot * 256:CSP + (cslot + 1) * 256],
                in_=ps[5][:, 0:256])), waits=tuple(w for w in e3w if w))
            tok = P.d("scalar", (lambda e, cslot=cslot, pair=pair, m2=m2:
                      e.dma_start(
                out=ctx_d[pair * 128:(pair + 1) * 128,
                          m2 * 256:(m2 + 1) * 256],
                in_=C[:, CSP + cslot * 256:CSP + (cslot + 1) * 256])),
                f"gCx{cslot}", waits=(spe_tok[m2],))
            csp_free[(pair, m2)] = tok
            ctx_written.append(tok)

        issue_patch(0)
        for m in range(_KQB):
            q0m = m * 256
            for tb in (2 * m - 1, 2 * m):
                if 1 <= tb < NT:
                    kra = P.c("vector", (lambda e, tb=tb: e.tensor_scalar(
                        B[:, KTMP + tb * 128:KTMP + (tb + 1) * 128],
                        B[:, KTMP + tb * 128:KTMP + (tb + 1) * 128],
                        cc(D03C), None, ALU.add)),
                        waits=(sc_last.get((m - 1, tb)), d03_tok, ksp[0]))
                    krh = P.c("scalar", (lambda e, tb=tb: e.activation(
                        out=Bb[:, KH + tb * 128:KH + (tb + 1) * 128],
                        in_=B[:, KTMP + tb * 128:KTMP + (tb + 1) * 128],
                        func=AF.Identity)),
                        waits=(kra, sc_last.get((m - 1, tb))))
                    krl = P.c("vector", (lambda e, tb=tb: e.tensor_tensor(
                        Bb[:, KL + tb * 128:KL + (tb + 1) * 128],
                        B[:, KTMP + tb * 128:KTMP + (tb + 1) * 128],
                        Bb[:, KH + tb * 128:KH + (tb + 1) * 128],
                        ALU.subtract)), waits=(krh,))
                    kre_tok[tb] = [krh, krl]
            pmap = {}
            for (tb, a0, pieces) in patch_list(m):
                pmap[tb] = pieces
            pt = pat_tok[(pair, m)]
            slot = m % 2
            lastpa = None
            for b in range(NT):
                for hh in range(2):
                    gsi = m * 32 + b * 2 + hh
                    u = gsi % 6
                    bank = ps[u % 3]
                    co = ((u // 3) % 2) * 256
                    mws = [relu_tok.get(gsi - 6), relu_tok.get(gsi - 3)] + list(qk_split)
                    if b in kre_tok:
                        mws += kre_tok[b]
                    mm = None
                    for ti, (qo, ko) in enumerate(
                            ((QH, KH), (QL, KH), (QH, KL))):
                        mm = P.c("tensor", (lambda e, bank=bank, co=co, b=b,
                                 hh=hh, q0m=q0m, qo=qo, ko=ko, ti=ti:
                                 e.matmul(
                            bank[:, co:co + 256],
                            Bb[:, ko + b * 128:ko + (b + 1) * 128][
                                hh * 64:(hh + 1) * 64, :],
                            Bb[:, qo + q0m:qo + q0m + 256][
                                hh * 64:(hh + 1) * 64, :],
                            start=(ti == 0), stop=(ti == 2))),
                            waits=tuple(w for w in mws if w)
                            if ti == 0 else ())
                    sc_tok[gsi] = mm
                    sc_last[(m, b)] = mm
                    rw = mm
                    if b in pmap:
                        for (cb, poff, var, qa, Qd, tqo) in pmap[b]:
                            pcol = PAT + slot * 1024 + cb + hh * 128
                            rw = P.c("tensor", (lambda e, bank=bank, co=co,
                                     tqo=tqo, Qd=Qd, poff=poff, pcol=pcol:
                                     e.matmul(
                                bank[:, co + tqo:co + tqo + Qd],
                                C[poff:poff + Qd, pcol:pcol + 128],
                                ids[poff:poff + Qd, poff:poff + Qd],
                                start=False, stop=True,
                                skip_group_check=True)),
                                waits=(pt,))
                        lastpa = rw
                    tso = TSL + (gsi % 8) * 256
                    rws = [rw, av_tok.get(gsi - 8)]
                    if gsi == 0:
                        rws.append(snb_ev)
                    if hh == 0:
                        rt = P.c("scalar", (lambda e, bank=bank, co=co,
                                 tso=tso: e.activation(
                            out=B[:, tso:tso + 256], in_=bank[:, co:co + 256],
                            func=AF.Relu, bias=cc(SNB), scale=1.0)),
                            waits=tuple(w for w in rws if w))
                        sq = P.c("scalar", (lambda e, tso=tso: e.activation(
                            out=B[:, tso:tso + 256], in_=B[:, tso:tso + 256],
                            func=AF.Square)), waits=(rt,))
                    else:
                        rt = P.c("vector", (lambda e, bank=bank, co=co,
                                 tso=tso: e.tensor_scalar(
                            B[:, tso:tso + 256], bank[:, co:co + 256],
                            cc(SNB), 0.0, ALU.add, ALU.max)),
                            waits=tuple(w for w in rws if w))
                        sq = P.c("vector", (lambda e, tso=tso:
                                 e.scalar_tensor_tensor(
                            B[:, tso:tso + 256], B[:, tso:tso + 256], 0.0,
                            B[:, tso:tso + 256], ALU.add, ALU.mult)),
                            waits=(rt,))
                    relu_tok[gsi] = rt
                    sq_tok[gsi] = sq
                if b >= 2:
                    emit_av(m, b - 2)
                if b == 5 and 2 * m + 2 < NT:
                    pe_build(2 * m + 2)
                elif b == 8 and 2 * m + 3 < NT:
                    pe_build(2 * m + 3)
                elif b == 9 and m + 1 < 8:
                    issue_patch(m + 1)
                if m > 0:
                    if b == 0:
                        emit_av(m - 1, 14)
                    elif b == 1:
                        emit_av(m - 1, 15)
                        ctx_evacs(m - 1)
                    elif b == 4:
                        t1_stage(m - 1)
                    elif b == 7:
                        norm_stage(m - 1)
                    elif b == 10:
                        t2_stage(m - 1)
                    elif b == 12:
                        spill_stage(m - 1)
            patch_used[(pair, m)] = lastpa
            padd_last[m] = lastpa
        if _KQB == 8:
            emit_av(7, 14)
            emit_av(7, 15)
            ctx_evacs(7)
            if pair < 7 and pair < _KNP - 1:
                pending_tail[0] = [lambda: t1_stage(7), lambda: norm_stage(7),
                                   lambda: t2_stage(7), lambda: spill_stage(7),
                                   lambda: None]
                def _set_pt(spe_tok=spe_tok):
                    return spe_tok[7]
                pending_tail[1] = _set_pt
            else:
                t1_stage(7)
                norm_stage(7)
                t2_stage(7)
                spill_stage(7)
                prev_tail = spe_tok[7]

    P.barrier()

    if _KPH < 2 or _KNP < 8:
        P.ops["gpsimd"].append((None, tuple(
            (sname, n) for sname, n in P.dma.items()
            if sname.startswith("g")), None))
        P.ops["sync"].append((None, tuple(
            (sname, n) for sname, n in P.dma.items()
            if sname.startswith("c")), None))
        return _finish(nc, P, sems, es)

    ctx_all0 = ("gCx0", P.dma["gCx0"])
    ctx_all1 = ("gCx1", P.dma["gCx1"])
    lastwo = {}
    lastxr = {}
    lastctx = {}
    bankfree2 = {}
    p2ev = []
    ln2_evs = []
    for qb in range(4):
        mmlast = None
        for cchunk in range(NDC):
            slot = cchunk % 2
            cws = [ctx_all0, ctx_all1,
                   lastctx.get((qb, cchunk - 2)) or (
                       lastctx.get((qb - 1, NDC - 2 + cchunk)) if cchunk < 2
                       and qb >= 1 else None)]
            cl = P.d("sync", (lambda e, cchunk=cchunk, qb=qb, slot=slot:
                     e.dma_start(
                out=C[:, CTX2 + slot * 512: CTX2 + (slot + 1) * 512],
                in_=ctx_d[cchunk * 128:(cchunk + 1) * 128,
                          qb * 512:(qb + 1) * 512])), f"cCr{slot}",
                waits=tuple(w for w in cws if w))
            wl = P.d("sync", (lambda e, cchunk=cchunk, slot=slot: e.dma_start(
                out=C[:, WOS + slot * 1024: WOS + (slot + 1) * 1024],
                in_=wot_d[cchunk * 128:(cchunk + 1) * 128, :])), f"cWo{slot}",
                waits=(lastwo.get(slot),))
            cxh, cxl = split("scalar",
                             W[:, 4096 + slot * 1024: 4096 + slot * 1024 + 512],
                             W[:, 4096 + slot * 1024 + 512:
                               4096 + slot * 1024 + 1024],
                             C[:, CTX2 + slot * 512: CTX2 + (slot + 1) * 512],
                             (cl, lastwo.get(slot)))
            woh, wol = split("scalar",
                             W[:, slot * 2048: slot * 2048 + 1024],
                             W[:, slot * 2048 + 1024: slot * 2048 + 2048],
                             C[:, WOS + slot * 1024: WOS + (slot + 1) * 1024],
                             (wl, lastwo.get(slot)))
            mws = [cxh, cxl, woh, wol]
            if qb >= 1 and cchunk == 0:
                mws.extend(p2ev[-2:])
            for dchunk in range(NDC):
                first = True
                for (wofs, xofs) in ((0, 4096), (1024, 4096), (0, 4608)):
                    mm = P.c("tensor", (lambda e, dchunk=dchunk, slot=slot,
                             cchunk=cchunk, wofs=wofs, xofs=xofs, first=first:
                             e.matmul(
                        ps[dchunk][:, 0:512],
                        W[:, slot * 2048 + wofs + dchunk * 128:
                          slot * 2048 + wofs + (dchunk + 1) * 128],
                        W[:, xofs + slot * 1024: xofs + slot * 1024 + 512],
                        start=(cchunk == 0 and first),
                        stop=(cchunk == NDC - 1 and wofs == 0 and xofs == 4608))),
                        waits=tuple(mws) if (dchunk == 0 and first) else ())
                    first = False
                    mmlast = mm
            lastctx[(qb, cchunk)] = mm
            lastwo[slot] = mm
        for dchunk in range(NDC):
            ev = P.c("scalar", (lambda e, dchunk=dchunk: e.activation(
                out=C[:, OFM + dchunk * 512: OFM + (dchunk + 1) * 512],
                in_=ps[dchunk][:, 0:512], func=AF.Identity,
                bias=cst[:, BOB + dchunk:BOB + dchunk + 1], scale=1.0)),
                waits=(mmlast, cC_tok) if dchunk == 0 else ())
            p2ev.append(ev)
        wo_ev_last = p2ev[-1]
        for u in range(4):
            tt = qb * 4 + u
            slot = tt % 2
            xl = P.d("sync", (lambda e, tt=tt, slot=slot: e.dma_start(
                out=C[:, XRE + slot * 1024: XRE + (slot + 1) * 1024],
                in_=x_d[tt * NP:(tt + 1) * NP, :])), f"cXr{slot}",
                waits=(lastxr.get(slot),))
            prev_add = lastxr.get(1 - slot)
            addlast = None
            for dchunk in range(NDC):
                bank = ps[dchunk % 2]
                tp = P.c("tensor", (lambda e, bank=bank, dchunk=dchunk, u=u:
                         e.transpose(
                    bank[:, 128:256],
                    C[:, OFM + dchunk * 512 + u * 128:
                      OFM + dchunk * 512 + u * 128 + 128], ids[:])),
                    waits=((wo_ev_last, prev_add) if prev_add else
                           (wo_ev_last,)) if dchunk == 0 else (
                        (addlast,) if dchunk >= 2 else ()))
                addlast = P.c("vector", (lambda e, bank=bank, tt=tt,
                              dchunk=dchunk, slot=slot: e.tensor_tensor(
                    B[:, tt * 1024 + dchunk * 128:
                      tt * 1024 + (dchunk + 1) * 128],
                    bank[:, 128:256],
                    C[:, XRE + slot * 1024 + dchunk * 128:
                      XRE + slot * 1024 + (dchunk + 1) * 128],
                    ALU.add)), waits=(tp, xl))
            lastxr[slot] = addlast
            x1ap = B[:, tt * 1024:(tt + 1) * 1024]
            r12 = ln_tile(x1ap, RED2, LNS2, XNT2,
                          tuple(ln2_evs[-2:]), (addlast,))
            ln2_evs += xn_transposes(tt, XNT2, r12, bankfree2)
            p2ev.append(ln2_evs[-3])
            p2ev.append(ln2_evs[-1])

    P.barrier()

    acc_tok = None
    lastwf = {}
    for cchunk in range(NDC):
        slot = cchunk % 2
        ldw = P.d("sync", (lambda e, cchunk=cchunk, slot=slot: e.dma_start(
            out=C[:, WFS + slot * 1024: WFS + (slot + 1) * 1024],
            in_=wft_d[cchunk * 128:(cchunk + 1) * 128, :])), f"cWf{slot}",
            waits=(lastwf.get(slot),))
        if cchunk == 0:
            acc_tok = P.c("vector", (lambda e: e.tensor_reduce(
                cc(ACC), C[:, WFS:WFS + 1024], AXX, ALU.add,
                apply_absolute_value=True)), waits=(ldw,))
        else:
            t = P.c("vector", (lambda e, slot=slot: e.tensor_reduce(
                C[:, RED3:RED3 + 1],
                C[:, WFS + slot * 1024: WFS + (slot + 1) * 1024],
                AXX, ALU.add, apply_absolute_value=True)),
                waits=(ldw, acc_tok))
            acc_tok = P.c("vector", (lambda e: e.tensor_add(
                cc(ACC), cc(ACC), C[:, RED3:RED3 + 1])), waits=(t,))
        lastwf[slot] = acc_tok
    tm = P.c("tensor", (lambda e: e.matmul(
        ps[6][0:1, 0:1], cc(ACC), cc(ONEC), start=True, stop=True)),
        waits=(acc_tok, onec_tok))
    sm = P.c("scalar", (lambda e: e.activation(
        out=sc1[0:1, 1:2], in_=ps[6][0:1, 0:1], func=AF.Identity,
        scale=1.0 / (D * D))), waits=(tm,))
    sx = P.c("vector", (lambda e: e.tensor_scalar(
        sc1[0:1, 2:3], sc1[0:1, 1:2], EPS_FF, None, ALU.max)), waits=(sm,))
    tb = P.c("tensor", (lambda e: e.matmul(
        ps[7][:, 0:1], onr[0:1, :], sc1[0:1, 2:3], start=True, stop=True)),
        waits=(sx,))
    sb_tok = P.c("vector", (lambda e: e.tensor_copy(
        out=cc(SSB), in_=ps[7][:, 0:1])), waits=(tb,))
    th_tok = P.c("vector", (lambda e: e.tensor_scalar(
        cc(THR), cc(SSB), 0.5, None, ALU.mult)), waits=(sb_tok,))
    nth_tok = P.c("vector", (lambda e: e.tensor_scalar(
        cc(NTHR), cc(SSB), -0.5, None, ALU.mult)), waits=(sb_tok,))
    tern_last = None
    for cchunk in range(NDC):
        slot = cchunk % 2
        ldw = P.d("sync", (lambda e, cchunk=cchunk, slot=slot: e.dma_start(
            out=C[:, WFS + slot * 1024: WFS + (slot + 1) * 1024],
            in_=wft_d[cchunk * 128:(cchunk + 1) * 128, :])), f"cWf{slot}",
            waits=(lastwf.get(slot),))
        wc = C[:, WFS + slot * 1024: WFS + (slot + 1) * 1024]
        g2 = P.c("vector", (lambda e, wc=wc: e.tensor_scalar(
            C[:, SCR:SCR + 1024], wc, cc(NTHR), None, ALU.is_lt)),
            waits=(nth_tok, th_tok, tern_last, ldw))
        tern_last = P.c("vector", (lambda e, wc=wc, cchunk=cchunk:
                        e.scalar_tensor_tensor(
            W[:, cchunk * 1024:(cchunk + 1) * 1024], wc, cc(THR),
            C[:, SCR:SCR + 1024], ALU.is_gt, ALU.subtract)),
            waits=(g2,))
        lastwf[slot] = tern_last
    tern_done = tern_last

    lastot = {}
    last_add3 = {}
    p3ev = []
    for qb in range(4):
        mmlast = None
        for cchunk in range(NDC):
            mws = [tern_done, p2ev[-1]]
            if qb >= 1 and cchunk == 0:
                mws.append(p3ev[-1])
                mws.append(last_add3.get(0))
                mws.append(last_add3.get(1))
            for dchunk in range(NDC):
                for ti, xf in enumerate((ah, al)):
                    mm = P.c("tensor", (lambda e, dchunk=dchunk, cchunk=cchunk,
                             qb=qb, xf=xf, ti=ti: e.matmul(
                        ps[dchunk][:, 0:512],
                        W[:, cchunk * 1024 + dchunk * 128:
                          cchunk * 1024 + (dchunk + 1) * 128],
                        xf(cchunk, qb * 512, 512),
                        start=(cchunk == 0 and ti == 0),
                        stop=(cchunk == NDC - 1 and ti == 1))),
                        waits=tuple(w for w in mws if w)
                        if (dchunk == 0 and ti == 0) else ())
                    mmlast = mm
        for dchunk in range(NDC):
            ev = P.c("scalar", (lambda e, dchunk=dchunk: e.activation(
                out=C[:, OFM3 + dchunk * 512: OFM3 + (dchunk + 1) * 512],
                in_=ps[dchunk][:, 0:512], func=AF.Identity,
                bias=0.0, scale=cc(SSB))),
                waits=(mmlast, sb_tok) if dchunk == 0 else ())
            p3ev.append(ev)
        ffn_ev_last = p3ev[-1]
        for u in range(4):
            tt = qb * 4 + u
            oslot = tt % 2
            prev_add3 = last_add3.get(1 - (tt % 2))
            addlast = None
            for dchunk in range(NDC):
                bank = ps[dchunk % 2]
                tp = P.c("tensor", (lambda e, bank=bank, dchunk=dchunk, u=u:
                         e.transpose(
                    bank[:, 128:256],
                    C[:, OFM3 + dchunk * 512 + u * 128:
                      OFM3 + dchunk * 512 + u * 128 + 128], ids[:])),
                    waits=((ffn_ev_last, prev_add3) if prev_add3 else
                           (ffn_ev_last,)) if dchunk == 0 else (
                        (addlast,) if dchunk >= 2 else ()))
                aws = [tp]
                if dchunk == 0:
                    aws.append(lastot.get(oslot))
                addlast = P.c("vector", (lambda e, bank=bank, tt=tt,
                              dchunk=dchunk, oslot=oslot: e.tensor_tensor(
                    C[:, OTL + oslot * 1024 + dchunk * 128:
                      OTL + oslot * 1024 + (dchunk + 1) * 128],
                    bank[:, 128:256],
                    B[:, tt * 1024 + dchunk * 128:
                      tt * 1024 + (dchunk + 1) * 128],
                    ALU.add)), waits=tuple(w for w in aws if w))
            ot = P.d("gpsimd", (lambda e, tt=tt, oslot=oslot: e.dma_start(
                out=out_d[tt * NP:(tt + 1) * NP, :],
                in_=C[:, OTL + oslot * 1024: OTL + (oslot + 1) * 1024])),
                f"gOut{oslot}", waits=(addlast,))
            lastot[oslot] = ot
            last_add3[tt % 2] = addlast

    P.ops["gpsimd"].append((None, (("gOut0", P.dma["gOut0"]),
                                   ("gOut1", P.dma["gOut1"])), None))
    return _finish(nc, P, sems, es)


def _finish(nc, P, sems, es):
    def runner(eng_name):
        def body(e):
            maxw = {}
            for fn, waits, inc in P.ops[eng_name]:
                for (s, v) in waits:
                    if maxw.get(s, -1) >= v:
                        continue
                    maxw[s] = v
                    e.wait_ge(sems[s], v)
                if fn is None:
                    continue
                ins = fn(e)
                if inc is not None:
                    ins.then_inc(sems[inc[0]], inc[1])
        return body

    with nc.Block() as block:
        block.sync(runner("sync"))
        block.tensor(runner("tensor"))
        block.vector(runner("vector"))
        block.scalar(runner("scalar"))
        block.gpsimd(runner("gpsimd"))

    es.close()
    return nc


_NC = None


def kernel(L0_attn_iQ, L0_attn_Wa, L0_attn_ba, L0_attn_rel_emb, L0_attn_sn_bias,
           L0_attn_Wo, L0_attn_bo, L0_ffn_weight):
    global _NC
    if _NC is None:
        _NC = build()
    iQ = np.asarray(L0_attn_iQ, dtype=np.float32)
    wat = np.ascontiguousarray(np.asarray(L0_attn_Wa, dtype=np.float32).T)
    ba = np.asarray(L0_attn_ba, dtype=np.float32)
    bat = np.ascontiguousarray(ba.reshape(24, NP).T)
    bavh = np.ascontiguousarray(ba[2048:].reshape(1, D))
    relT0 = np.asarray(L0_attn_rel_emb, dtype=np.float32).T
    relT = np.ascontiguousarray(np.concatenate([relT0, relT0], axis=0))
    snb = np.asarray(L0_attn_sn_bias, dtype=np.float32).reshape(1, 1)
    wot = np.ascontiguousarray(np.asarray(L0_attn_Wo, dtype=np.float32).T)
    bot = np.ascontiguousarray(np.asarray(L0_attn_bo, dtype=np.float32)
                               .reshape(8, NP).T)
    wft = np.ascontiguousarray(np.asarray(L0_ffn_weight, dtype=np.float32).T)
    ident = np.eye(NP, dtype=np.float32)
    onesr = np.ones((1, NP), dtype=np.float32)
    in_maps = []
    for c in range(8):
        in_maps.append({
            "x": np.ascontiguousarray(iQ[c]),
            "wat": wat, "bat": bat, "bavd": bavh, "relT": relT, "snb": snb,
            "wot": wot, "bot": bot, "wft": wft,
            "ident": ident, "onesr": onesr,
        })
    res = run_bass_kernel_spmd(_NC, in_maps, core_ids=list(range(8)))
    out = np.stack([np.asarray(res.results[c]["out"]) for c in range(8)], axis=0)
    return out
